# version 12 (final)
# speedup vs baseline: 1.0172x; 1.0070x over previous
import numpy as np
import concourse.bass as bass
import concourse.mybir as mybir
from concourse.bass_utils import run_bass_kernel_spmd

F32 = mybir.dt.float32
AF = mybir.ActivationFunctionType
ALU = mybir.AluOpType
AX = mybir.AxisListType

NCORES = 8
D = 1024
T = 2048
B, L = 4, 4096

ENGINES = ("sync", "tensor", "vector", "scalar", "gpsimd")


class Prog:

    def __init__(self, nc: bass.Bass):
        self.nc = nc
        self.phases = []
        self._cur = None
        self.phase()

    def phase(self, engines=None):
        self._cur = {e: [] for e in ENGINES}
        self._cur["_eng"] = tuple(engines) if engines else ENGINES
        self.phases.append(self._cur)

    def on(self, engine, fn):
        self._cur[engine].append(("c", fn))

    def dma(self, engine, out, in_, **kw):
        def fn(eng, _o=out, _i=in_, _kw=kw):
            return eng.dma_start(out=_o, in_=_i, **_kw)
        self._cur[engine].append(("d", fn))

    def build(self):
        nc = self.nc
        bsem = {e: nc.alloc_semaphore(f"b_{e}") for e in ENGINES}
        csem = {e: nc.alloc_semaphore(f"c_{e}") for e in ENGINES}
        dsem = {e: nc.alloc_semaphore(f"d_{e}") for e in ENGINES}
        cum = {e: 0 for e in ENGINES}
        pre = []
        for ph in self.phases:
            pre.append(dict(cum))
            for e in ph["_eng"]:
                cum[e] += 1
        total = cum

        def replay(eng, E):
            dmacum = 0
            ccum = 0
            for p, ph in enumerate(self.phases):
                if E not in ph["_eng"]:
                    assert not ph[E], f"phase {p}: {E} has actions but excluded"
                    continue
                for o in ph["_eng"]:
                    if o != E and pre[p][o] > 0:
                        eng.wait_ge(bsem[o], pre[p][o])
                last_c = None
                ndma = 0
                for kind, fn in ph[E]:
                    instr = fn(eng)
                    if kind == "d":
                        instr.then_inc(dsem[E], 16)
                        dmacum += 16
                        ndma += 1
                    else:
                        if instr is not None:
                            last_c = instr
                if last_c is not None:
                    last_c.then_inc(csem[E], 1)
                    ccum += 1
                    eng.wait_ge(csem[E], ccum)
                if ndma:
                    eng.wait_ge(dsem[E], dmacum)
                eng.sem_inc(bsem[E], 1)
            for o in ENGINES:
                if o != E and total[o] > 0:
                    eng.wait_ge(bsem[o], total[o])

        with nc.Block() as block:
            @block.sync
            def _(eng):
                replay(eng, "sync")

            @block.tensor
            def _(eng):
                replay(eng, "tensor")

            @block.vector
            def _(eng):
                replay(eng, "vector")

            @block.scalar
            def _(eng):
                replay(eng, "scalar")

            @block.gpsimd
            def _(eng):
                replay(eng, "gpsimd")
        return nc


def to_fm(x):
    t, f = x.shape
    return np.ascontiguousarray(x.T.reshape(f // 128, 128, t))


def from_fm(x):
    nf, p, t = x.shape
    return np.ascontiguousarray(x.reshape(nf * p, t).T)


def to_tm(x, pad_to=None):
    t, f = x.shape
    return np.ascontiguousarray(x.reshape(t // 128, 128, f))


def from_tm(x):
    nt, p, f = x.shape
    return np.ascontiguousarray(x.reshape(nt * p, f))


def lhsT_tiles(w):
    kin, mout = w.shape
    kpad = (-kin) % 128
    if kpad:
        w = np.concatenate([w, np.zeros((kpad, mout), w.dtype)], 0)
    return np.ascontiguousarray(w.reshape((kin + kpad) // 128, 128, mout))


def shard_tokens(x_bld):
    out = []
    for c in range(NCORES):
        b, h = c // 2, c % 2
        out.append(np.ascontiguousarray(x_bld[b, h * T:(h + 1) * T]))
    return out


def unshard_tokens(parts):
    out = np.empty((B, L, parts[0].shape[-1]), parts[0].dtype)
    for c in range(NCORES):
        b, h = c // 2, c % 2
        out[b, h * T:(h + 1) * T] = parts[c]
    return out


class Ctx:

    def __init__(self, nc, n_psum=8, psum_n=512):
        self.nc = nc
        self.psum = [nc.alloc_psum_tensor(f"ps{i}", [128, psum_n], F32).ap()
                     for i in range(n_psum)]
        self._rot = list(range(n_psum))
        self._pi = 0

    def bank(self):
        p = self.psum[self._rot[self._pi % len(self._rot)]]
        self._pi += 1
        return p

    def reserve(self):
        i = self._rot.pop()
        return self.psum[i]


def emit_proj(prog, ctx, out_tiles, w_sb, x_tiles, kin, mout, N,
              bias_sb=None, func=None, scale=1.0, evac="scalar",
              add_into=None, mul_sb=None, nchunk=512, evac_dtype=None,
              accum_banks=True):
    nc = prog.nc
    ktiles = len(x_tiles)
    mtiles = (mout + 127) // 128
    pend = []
    for mi in range(mtiles):
        mw = min(128, mout - mi * 128)
        for n0 in range(0, N, nchunk):
            nw = min(nchunk, N - n0)
            bank = ctx.bank()
            prog.phase()
            def mk(eng, _mi=mi, _mw=mw, _n0=n0, _nw=nw, _bank=bank):
                ins = None
                for k in range(ktiles):
                    ins = eng.matmul(_bank[0:_mw, 0:_nw],
                                     w_sb(k, _mi)[:, 0:_mw],
                                     x_tiles[k][:, _n0:_n0 + _nw],
                                     start=(k == 0), stop=(k == ktiles - 1))
                return ins
            prog.on("tensor", mk)
            _flush(prog, out_tiles, bias_sb, func, scale, evac, add_into,
                   pend)
            pend = [(bank, mi, mw, n0, nw)]
    prog.phase()
    _flush(prog, out_tiles, bias_sb, func, scale, evac, add_into, pend)


def _flush(prog, out_tiles, bias_sb, func, scale, evac, add_into, pend):
    for (bank, mi, mw, n0, nw) in pend:
        if add_into:
            def fv(eng, _b=bank, _mi=mi, _mw=mw, _n0=n0, _nw=nw):
                return eng.tensor_add(out_tiles[_mi][0:_mw, _n0:_n0 + _nw],
                                      out_tiles[_mi][0:_mw, _n0:_n0 + _nw],
                                      _b[0:_mw, 0:_nw])
            prog.on("vector", fv)
        else:
            f = AF.Copy if func is None else func
            def fs(eng, _b=bank, _mi=mi, _mw=mw, _n0=n0, _nw=nw, _f=f):
                bias = 0.0 if bias_sb is None else bias_sb[_mi][0:_mw, :]
                return eng.activation(out_tiles[_mi][0:_mw, _n0:_n0 + _nw],
                                      _b[0:_mw, 0:_nw], _f,
                                      bias=bias, scale=scale)
            prog.on(evac, fs)


def sb(nc, name, p, f, dtype=F32):
    return nc.alloc_sbuf_tensor(name, [p, f], dtype).ap()


def build_ln():
    nc = bass.Bass("TRN2", target_bir_lowering=False)
    x = nc.dram_tensor("x", [8, 128, T], F32, kind="ExternalInput").ap()
    xn = nc.dram_tensor("xn", [8, 128, T], F32, kind="ExternalOutput").ap()
    prog = Prog(nc)
    ctx = Ctx(nc)
    xt = [sb(nc, f"x{i}", 128, T) for i in range(8)]
    scr = [sb(nc, f"scr{i}", 128, T) for i in range(2)]
    ones1 = sb(nc, "ones1", 128, 1)
    onesr = sb(nc, "onesr", 1, 128)
    mean = sb(nc, "mean", 1, T)
    exsq = sb(nc, "exsq", 1, T)
    m2 = sb(nc, "m2", 1, T)
    var = sb(nc, "var", 1, T)
    rstd = sb(nc, "rstd", 1, T)
    mb = sb(nc, "mb", 128, T)
    rb = sb(nc, "rb", 128, T)

    for i in range(8):
        prog.dma("sync", xt[i][:], x[i])
    prog.on("gpsimd", lambda e: e.memset(ones1[:], 1.0))
    prog.on("gpsimd", lambda e: e.memset(onesr[:], 1.0))

    prog.phase()
    sbank = [ctx.bank() for _ in range(4)]
    qbank = [ctx.bank() for _ in range(4)]
    def mm_sum(eng):
        ins = None
        for c in range(4):
            for k in range(8):
                ins = eng.matmul(sbank[c][0:1, :], ones1[:, 0:1],
                                 xt[k][:, c * 512:(c + 1) * 512],
                                 start=(k == 0), stop=(k == 7))
        return ins
    prog.on("tensor", mm_sum)
    prog.on("scalar", lambda e: e.activation(scr[0][:], xt[0][:], AF.Square))

    for k in range(1, 9):
        prog.phase()
        if k < 8:
            prog.on("scalar", lambda e, _k=k: e.activation(
                scr[_k % 2][:], xt[_k][:], AF.Square))
        def mm_sq(eng, _k=k - 1):
            ins = None
            for c in range(4):
                ins = eng.matmul(qbank[c][0:1, :], ones1[:, 0:1],
                                 scr[_k % 2][:, c * 512:(c + 1) * 512],
                                 start=(_k == 0), stop=(_k == 7))
            return ins
        prog.on("tensor", mm_sq)

    prog.phase()
    for c in range(4):
        prog.on("scalar", lambda e, _c=c: e.activation(
            mean[:, _c * 512:(_c + 1) * 512], sbank[_c][0:1, :], AF.Copy,
            scale=1.0 / D))

    prog.phase()
    for c in range(4):
        prog.on("scalar", lambda e, _c=c: e.activation(
            exsq[:, _c * 512:(_c + 1) * 512], qbank[_c][0:1, :], AF.Copy,
            scale=1.0 / D))
    prog.on("vector", lambda e: e.tensor_mul(m2[:], mean[:], mean[:]))
    prog.phase()
    prog.on("vector", lambda e: e.tensor_sub(var[:], exsq[:], m2[:]))
    prog.phase()
    prog.on("vector", lambda e: e.tensor_scalar_add(var[:], var[:], 1e-5))
    prog.phase()
    prog.on("scalar", lambda e: e.activation(var[:], var[:], AF.Sqrt))
    prog.phase()
    prog.on("vector", lambda e: e.reciprocal(rstd[:], var[:]))

    prog.phase()
    bbank = [ctx.bank() for _ in range(8)]
    def mm_b(eng):
        ins = None
        for c in range(4):
            ins = eng.matmul(bbank[c][:, :], onesr[0:1, :],
                             mean[:, c * 512:(c + 1) * 512], start=True, stop=True)
            ins = eng.matmul(bbank[4 + c][:, :], onesr[0:1, :],
                             rstd[:, c * 512:(c + 1) * 512], start=True, stop=True)
        return ins
    prog.on("tensor", mm_b)
    prog.phase()
    for c in range(4):
        prog.on("scalar", lambda e, _c=c: e.activation(
            mb[:, _c * 512:(_c + 1) * 512], bbank[_c][:, :], AF.Copy))
        prog.on("vector", lambda e, _c=c: e.tensor_copy(
            rb[:, _c * 512:(_c + 1) * 512], bbank[4 + _c][:, :]))

    prog.phase()
    for i in range(8):
        prog.on("vector", lambda e, _i=i: e.tensor_sub(xt[_i][:], xt[_i][:], mb[:]))
    prog.phase()
    for i in range(8):
        prog.on("vector", lambda e, _i=i: e.tensor_mul(xt[_i][:], xt[_i][:], rb[:]))

    prog.phase()
    for i in range(8):
        prog.dma("gpsimd", xn[i], xt[i][:])
    return prog.build()


_nc_cache = {}


def get_prog(name, builder, *args):
    key = (name,) + tuple(args)
    if key not in _nc_cache:
        _nc_cache[key] = builder(*args)
    return _nc_cache[key]


def run8(nc, in_maps, **kw):
    res = run_bass_kernel_spmd(nc, in_maps, core_ids=list(range(NCORES)), **kw)
    return res.results


def stage_ln(x_parts_fm):
    nc = get_prog("ln", build_ln)
    res = run8(nc, [{"x": xp} for xp in x_parts_fm])
    return [r["xn"] for r in res]


ACTS = {"none": AF.Copy, "relu": AF.Relu, "silu": AF.Silu, "gelu": AF.Gelu,
        "sigmoid": AF.Sigmoid, "exp": AF.Exp}


def build_proj(kin, mout, act, has_bias, has_resid, stream_out, Tloc,
               stream_w=False):
    nc = bass.Bass("TRN2", target_bir_lowering=False)
    ktiles = (kin + 127) // 128
    mtiles = (mout + 127) // 128
    x = nc.dram_tensor("x", [ktiles, 128, Tloc], F32, kind="ExternalInput").ap()
    w = nc.dram_tensor("w", [ktiles, 128, mout], F32, kind="ExternalInput").ap()
    if has_bias:
        bdr = nc.dram_tensor("b", [mtiles, 128, 1], F32, kind="ExternalInput").ap()
    if has_resid:
        rdr = nc.dram_tensor("r", [mtiles, 128, Tloc], F32, kind="ExternalInput").ap()
    out = nc.dram_tensor("out", [mtiles, 128, Tloc], F32, kind="ExternalOutput").ap()

    prog = Prog(nc)
    ctx = Ctx(nc)
    xt = [sb(nc, f"x{i}", 128, Tloc) for i in range(ktiles)]
    if stream_w:
        wt2 = [[sb(nc, f"w{j}_{i}", 128, 128) for i in range(ktiles)]
               for j in range(2)]
    else:
        wt = [sb(nc, f"w{i}", 128, mout) for i in range(ktiles)]
    bt = [sb(nc, f"b{i}", 128, 1) for i in range(mtiles)] if has_bias else None
    n_ot = 2 if stream_out else mtiles
    ot = [sb(nc, f"o{i}", 128, Tloc) for i in range(n_ot)]
    rt = ([sb(nc, f"r{i}", 128, Tloc) for i in range(2 if stream_out else mtiles)]
          if has_resid else None)

    for i in range(ktiles):
        prog.dma("sync", xt[i][:], x[i])
        if not stream_w:
            prog.dma("sync" if i % 2 else "scalar", wt[i][:], w[i])
    if has_bias:
        for i in range(mtiles):
            prog.dma("sync", bt[i][:], bdr[i])
    if has_resid and not stream_out:
        for i in range(mtiles):
            prog.dma("scalar", rt[i][:], rdr[i])

    func = ACTS[act]
    NCH = 512
    nch = (Tloc + NCH - 1) // NCH
    pend = []

    def flush():
        for (bank, mi, mw, n0, nw) in pend:
            oslot = (mi % 2) if stream_out else mi
            if has_resid:
                def fv(e, _b=bank, _mi=mi, _mw=mw, _n0=n0, _nw=nw, _os=oslot):
                    if has_bias:
                        return e.scalar_tensor_tensor(
                            ot[_os][0:_mw, _n0:_n0 + _nw], _b[0:_mw, 0:_nw],
                            bt[_mi][0:_mw, :], rt[_os][0:_mw, _n0:_n0 + _nw],
                            ALU.add, ALU.add)
                    return e.tensor_add(ot[_os][0:_mw, _n0:_n0 + _nw],
                                        _b[0:_mw, 0:_nw],
                                        rt[_os][0:_mw, _n0:_n0 + _nw])
                prog.on("vector", fv)
            elif act == "none" and has_bias:
                def fv2(e, _b=bank, _mi=mi, _mw=mw, _n0=n0, _nw=nw, _os=oslot):
                    return e.tensor_scalar_add(ot[_os][0:_mw, _n0:_n0 + _nw],
                                               _b[0:_mw, 0:_nw],
                                               bt[_mi][0:_mw, :])
                prog.on("vector", fv2)
            else:
                def fs(e, _b=bank, _mi=mi, _mw=mw, _n0=n0, _nw=nw, _os=oslot):
                    bias = bt[_mi][0:_mw, :] if has_bias else 0.0
                    return e.activation(ot[_os][0:_mw, _n0:_n0 + _nw],
                                        _b[0:_mw, 0:_nw], func, bias=bias)
                prog.on("scalar", fs)

    for mi in range(mtiles):
        mw = min(128, mout - mi * 128)
        if (stream_out and has_resid) or stream_w:
            prog.phase()
            if stream_out and has_resid:
                prog.dma("scalar", rt[mi % 2][0:mw, :], rdr[mi][0:mw, :])
            if stream_w:
                for k in range(ktiles):
                    prog.dma("sync", wt2[mi % 2][k][:, 0:mw],
                             w[k][:, mi * 128:mi * 128 + mw])
        for n0 in range(0, Tloc, NCH):
            nw = min(NCH, Tloc - n0)
            bank = ctx.bank()
            prog.phase()
            def mk(e, _mi=mi, _mw=mw, _n0=n0, _nw=nw, _bank=bank):
                ins = None
                for k in range(ktiles):
                    lhs = (wt2[_mi % 2][k][:, 0:_mw] if stream_w
                           else wt[k][:, _mi * 128:_mi * 128 + _mw])
                    ins = e.matmul(_bank[0:_mw, 0:_nw], lhs,
                                   xt[k][:, _n0:_n0 + _nw],
                                   start=(k == 0), stop=(k == ktiles - 1))
                return ins
            prog.on("tensor", mk)
            flush()
            pend = [(bank, mi, mw, n0, nw)]
        if stream_out:
            prog.phase()
            flush()
            pend = []
            prog.phase()
            prog.dma("gpsimd", out[mi][0:mw, :], ot[mi % 2][0:mw, :])
    prog.phase()
    flush()
    pend = []
    if not stream_out:
        prog.phase()
        for mi in range(mtiles):
            mw = min(128, mout - mi * 128)
            prog.dma("gpsimd", out[mi][0:mw, :], ot[mi][0:mw, :])
    return prog.build()


def pad_feat(x, mult=128):
    t, f = x.shape
    p = (-f) % mult
    if p:
        x = np.concatenate([x, np.zeros((t, p), x.dtype)], 1)
    return x


def stage_proj(x_parts_fm, W, act="none", bias=None, resid_parts_fm=None,
               stream_out=False, stream_w=False):
    kin, mout = W.shape
    Tloc = x_parts_fm[0].shape[2]
    wt = lhsT_tiles(W.astype(np.float32))
    mtiles = (mout + 127) // 128
    nc = get_prog("proj", build_proj, kin, mout, act, bias is not None,
                  resid_parts_fm is not None, stream_out, Tloc, stream_w)
    in_maps = []
    for c in range(NCORES):
        m = {"x": x_parts_fm[c], "w": wt}
        if bias is not None:
            bp = np.zeros(mtiles * 128, np.float32)
            bp[:mout] = bias
            m["b"] = np.ascontiguousarray(bp.reshape(mtiles, 128, 1))
        if resid_parts_fm is not None:
            m["r"] = resid_parts_fm[c]
        in_maps.append(m)
    res = run8(nc, in_maps)
    return [r["out"] for r in res]


def build_sincos():
    nc = bass.Bass("TRN2", target_bir_lowering=False)
    q = nc.dram_tensor("q", [8, 128, T], F32, kind="ExternalInput").ap()
    k = nc.dram_tensor("k", [8, 128, T], F32, kind="ExternalInput").ap()
    sinb = nc.dram_tensor("sinb", [128, T], F32, kind="ExternalInput").ap()
    cosb = nc.dram_tensor("cosb", [128, T], F32, kind="ExternalInput").ap()
    outs = {n: nc.dram_tensor(n, [8, 128, T], F32, kind="ExternalOutput").ap()
            for n in ("qs", "qc", "ks", "kc")}
    prog = Prog(nc)
    qt = [sb(nc, f"q{i}", 128, T) for i in range(8)]
    kt = [sb(nc, f"k{i}", 128, T) for i in range(8)]
    st_ = sb(nc, "sin", 128, T)
    ct_ = sb(nc, "cos", 128, T)
    res = {n: sb(nc, f"r_{n}", 128, T) for n in ("qs", "qc", "ks", "kc")}
    for i in range(8):
        prog.dma("sync", qt[i][:], q[i])
        prog.dma("sync", kt[i][:], k[i])
    prog.dma("scalar", st_[:], sinb)
    prog.dma("scalar", ct_[:], cosb)
    for i in range(8):
        prog.phase()
        prog.on("vector", lambda e, _i=i: e.tensor_mul(res["qs"][:], qt[_i][:], st_[:]))
        prog.on("gpsimd", lambda e, _i=i: e.tensor_mul(res["qc"][:], qt[_i][:], ct_[:]))
        prog.phase()
        prog.on("vector", lambda e, _i=i: e.tensor_mul(res["ks"][:], kt[_i][:], st_[:]))
        prog.on("gpsimd", lambda e, _i=i: e.tensor_mul(res["kc"][:], kt[_i][:], ct_[:]))
        prog.phase()
        for n in ("qs", "qc", "ks", "kc"):
            prog.dma("gpsimd" if n in ("qs", "qc") else "sync", outs[n][i], res[n][:])
        prog.phase()
    return prog.build()


def stage_sincos(q_fm, k_fm, sinb, cosb):
    nc = get_prog("sincos", build_sincos)
    res = run8(nc, [{"q": q_fm[c], "k": k_fm[c], "sinb": sinb[c], "cosb": cosb[c]}
                    for c in range(NCORES)])
    return ({n: [r[n] for r in res] for n in ("qs", "qc", "ks", "kc")})


def build_mix0():
    nc = bass.Bass("TRN2", target_bir_lowering=False)
    qs = nc.dram_tensor("qs", [8, 64, L], F32, kind="ExternalInput").ap()
    qc = nc.dram_tensor("qc", [8, 64, L], F32, kind="ExternalInput").ap()
    ksT = nc.dram_tensor("ksT", [8, 128, 32 * 64], F32, kind="ExternalInput").ap()
    kcT = nc.dram_tensor("kcT", [8, 128, 32 * 64], F32, kind="ExternalInput").ap()
    vT1 = nc.dram_tensor("vT1", [8, 128, 32 * 65], F32, kind="ExternalInput").ap()
    out = nc.dram_tensor("out", [8, 64, L], F32, kind="ExternalOutput").ap()
    prog = Prog(nc)
    ctx = Ctx(nc)
    qs_sb = sb(nc, "qs_s", 64, L)
    qc_sb = sb(nc, "qc_s", 64, L)
    ksT_sb = sb(nc, "ksT_s", 128, 32 * 64)
    kcT_sb = sb(nc, "kcT_s", 128, 32 * 64)
    vT1_sb = sb(nc, "vT1_s", 128, 32 * 65)
    kvt = sb(nc, "kvt", 64, 65)
    kvb = sb(nc, "kvb", 64, 65)
    att = sb(nc, "att", 64, L)
    den = sb(nc, "den", 1, L)
    zb = sb(nc, "zb", 64, L)
    ones64 = sb(nc, "ones64", 1, 64)
    osb = sb(nc, "osb", 64, L)

    prog.on("gpsimd", lambda e: e.memset(ones64[:], 1.0))
    for h in range(8):
        prog.phase()
        prog.dma("sync", qs_sb[:], qs[h])
        prog.dma("sync", qc_sb[:], qc[h])
        prog.dma("scalar", ksT_sb[:], ksT[h])
        prog.dma("scalar", kcT_sb[:], kcT[h])
        prog.dma("gpsimd", vT1_sb[:], vT1[h])
        prog.phase()
        bt_, bb_ = ctx.bank(), ctx.bank()
        def mm_kv(e, _bt=bt_, _bb=bb_):
            ins = None
            for kt in range(32):
                ins = e.matmul(_bt[0:64, 0:65], ksT_sb[:, kt * 64:(kt + 1) * 64],
                               vT1_sb[:, kt * 65:(kt + 1) * 65],
                               start=(kt == 0), stop=(kt == 31))
            for kt in range(32):
                ins = e.matmul(_bb[0:64, 0:65], kcT_sb[:, kt * 64:(kt + 1) * 64],
                               vT1_sb[:, kt * 65:(kt + 1) * 65],
                               start=(kt == 0), stop=(kt == 31))
            return ins
        prog.on("tensor", mm_kv)
        prog.phase()
        prog.on("vector", lambda e, _b=bt_: e.tensor_copy(kvt[:], _b[0:64, 0:65]))
        prog.on("vector", lambda e, _b=bb_: e.tensor_copy(kvb[:], _b[0:64, 0:65]))
        prog.phase()
        abanks = [ctx.bank() for _ in range(8)]
        def mm_att(e, _ab=abanks):
            ins = None
            for c in range(8):
                ins = e.matmul(_ab[c][0:65, :], kvt[:, :],
                               qs_sb[:, c * 512:(c + 1) * 512], start=True, stop=False)
                ins = e.matmul(_ab[c][0:65, :], kvb[:, :],
                               qc_sb[:, c * 512:(c + 1) * 512], start=False, stop=True)
            return ins
        prog.on("tensor", mm_att)
        prog.phase()
        for c in range(8):
            prog.on("scalar", lambda e, _c=c, _ab=abanks: e.activation(
                att[:, _c * 512:(_c + 1) * 512], _ab[_c][0:64, :], AF.Copy))
            prog.on("vector", lambda e, _c=c, _ab=abanks: e.tensor_copy(
                den[:, _c * 512:(_c + 1) * 512], _ab[_c][64:65, :]))
        prog.phase()
        prog.on("vector", lambda e: e.tensor_scalar_max(den[:], den[:], 1e-4))
        prog.phase()
        prog.on("vector", lambda e: e.reciprocal(den[:], den[:]))
        prog.phase()
        zbanks = [ctx.bank() for _ in range(8)]
        def mm_zb(e, _zb=zbanks):
            ins = None
            for c in range(8):
                ins = e.matmul(_zb[c][0:64, :], ones64[0:1, :],
                               den[0:1, c * 512:(c + 1) * 512], start=True, stop=True)
            return ins
        prog.on("tensor", mm_zb)
        prog.phase()
        for c in range(8):
            prog.on("scalar", lambda e, _c=c, _zb=zbanks: e.activation(
                zb[:, _c * 512:(_c + 1) * 512], _zb[_c][0:64, :], AF.Copy))
        prog.phase()
        prog.on("vector", lambda e: e.tensor_mul(osb[:], att[:], zb[:]))
        prog.phase()
        prog.dma("gpsimd", out[h], osb[:])
        prog.phase()
    return prog.build()


def stage_mix0(qs_p, qc_p, ks_p, kc_p, v_p):
    qs = unshard_tokens([from_fm(p) for p in qs_p])
    qc = unshard_tokens([from_fm(p) for p in qc_p])
    ks = unshard_tokens([from_fm(p) for p in ks_p])
    kc = unshard_tokens([from_fm(p) for p in kc_p])
    v = unshard_tokens([from_fm(p) for p in v_p])
    in_maps = []
    for c in range(NCORES):
        m = {k: [] for k in ("qs", "qc", "ksT", "kcT", "vT1")}
        for j in range(8):
            g = c * 8 + j
            b, h = g // 16, g % 16
            sl = slice(h * 64, (h + 1) * 64)
            m["qs"].append(qs[b, :, sl].T)
            m["qc"].append(qc[b, :, sl].T)
            m["ksT"].append(ks[b, :, sl].reshape(32, 128, 64).transpose(1, 0, 2).reshape(128, -1))
            m["kcT"].append(kc[b, :, sl].reshape(32, 128, 64).transpose(1, 0, 2).reshape(128, -1))
            vv = np.concatenate([v[b, :, sl], np.ones((L, 1), np.float32)], 1)
            m["vT1"].append(vv.reshape(32, 128, 65).transpose(1, 0, 2).reshape(128, -1))
        in_maps.append({k: np.ascontiguousarray(np.stack(vl)) for k, vl in m.items()})
    nc = get_prog("mix0", build_mix0)
    res = run8(nc, in_maps)
    attn = np.zeros((B, L, D), np.float32)
    for c in range(NCORES):
        for j in range(8):
            g = c * 8 + j
            b, h = g // 16, g % 16
            attn[b, :, h * 64:(h + 1) * 64] = res[c]["out"][j].T
    return attn


def l0_attn_apply(inp, x_fm, xn_fm):
    WqT = np.ascontiguousarray(inp["L0_attn_Wq"].T)
    WkT = np.ascontiguousarray(inp["L0_attn_Wk"].T)
    WvT = np.ascontiguousarray(inp["L0_attn_Wv"].T)
    WoT = np.ascontiguousarray(inp["L0_attn_Wo"].T)
    q = stage_proj(xn_fm, WqT, act="relu", bias=inp["L0_attn_bq"])
    k = stage_proj(xn_fm, WkT, act="relu", bias=inp["L0_attn_bk"])
    v = stage_proj(xn_fm, WvT, act="none", bias=inp["L0_attn_bv"])
    idx = (np.pi / 2 * np.arange(1, L + 1, dtype=np.float64) / L)
    sin = np.sin(idx).astype(np.float32)
    cos = np.cos(idx).astype(np.float32)
    sinb, cosb = [], []
    for c in range(NCORES):
        h = c % 2
        sinb.append(np.ascontiguousarray(np.tile(sin[h * T:(h + 1) * T], (128, 1))))
        cosb.append(np.ascontiguousarray(np.tile(cos[h * T:(h + 1) * T], (128, 1))))
    sc = stage_sincos(q, k, sinb, cosb)
    attn = stage_mix0(sc["qs"], sc["qc"], sc["ks"], sc["kc"], v)
    attn_fm = [to_fm(p) for p in shard_tokens(attn)]
    return stage_proj(attn_fm, WoT, act="none", bias=inp["L0_attn_bo"],
                      resid_parts_fm=x_fm, stream_out=True)


def build_gate0():
    nc = bass.Bass("TRN2", target_bir_lowering=False)
    x = nc.dram_tensor("x", [8, 128, T], F32, kind="ExternalInput").ap()
    gw = nc.dram_tensor("gw", [8, 128, 8], F32, kind="ExternalInput").ap()
    gb = nc.dram_tensor("gb", [128, 8], F32, kind="ExternalInput").ap()
    comb = nc.dram_tensor("comb", [16, 128, 8], F32, kind="ExternalOutput").ap()
    prog = Prog(nc)
    ctx = Ctx(nc)
    xt = [sb(nc, f"x{i}", 128, T) for i in range(8)]
    gwt = sb(nc, "gwt", 128, 8 * 8)
    gbt = sb(nc, "gbt", 128, 8)
    s_sb = [sb(nc, f"s{i}", 128, 8) for i in range(16)]
    c_sb = [sb(nc, f"c{i}", 128, 8) for i in range(16)]
    t1 = sb(nc, "t1", 128, 8)
    t2 = sb(nc, "t2", 128, 8)
    mx = sb(nc, "mx", 128, 1)
    sm = sb(nc, "sm", 128, 1)
    for i in range(8):
        prog.dma("sync", xt[i][:], x[i])
        prog.dma("scalar", gwt[:, i * 8:(i + 1) * 8], gw[i])
    prog.dma("scalar", gbt[:], gb)
    for ti in range(16):
        bank = ctx.bank()
        prog.phase()
        def mk(e, _ti=ti, _b=bank):
            ins = None
            for k in range(8):
                ins = e.matmul(_b[0:128, 0:8], xt[k][:, _ti * 128:(_ti + 1) * 128],
                               gwt[:, k * 8:(k + 1) * 8],
                               start=(k == 0), stop=(k == 7))
            return ins
        prog.on("tensor", mk)
        prog.phase()
        prog.on("vector", lambda e, _ti=ti, _b=bank: e.tensor_copy(s_sb[_ti][:], _b[0:128, 0:8]))
        prog.phase()
        prog.on("vector", lambda e, _ti=ti: e.reduce_max(mx[:], s_sb[_ti][:], axis=AX.X))
        prog.phase()
        prog.on("vector", lambda e: e.tensor_scalar_mul(mx[:], mx[:], -1.0))
        prog.phase()
        prog.on("scalar", lambda e, _ti=ti: e.activation(s_sb[_ti][:], s_sb[_ti][:], AF.Exp, bias=mx[:]))
        prog.phase()
        prog.on("vector", lambda e, _ti=ti: e.tensor_reduce(sm[:], s_sb[_ti][:], axis=AX.X, op=ALU.add))
        prog.phase()
        prog.on("vector", lambda e: e.reciprocal(sm[:], sm[:]))
        prog.phase()
        prog.on("vector", lambda e, _ti=ti: e.tensor_scalar_mul(s_sb[_ti][:], s_sb[_ti][:], sm[:]))
        prog.phase()
        prog.on("vector", lambda e, _ti=ti: e.tensor_add(t1[:], s_sb[_ti][:], gbt[:]))
        prog.phase()
        prog.on("vector", lambda e: e.reduce_max(mx[:], t1[:], axis=AX.X))
        prog.phase()
        prog.on("vector", lambda e: e.tensor_scalar(t2[:], t1[:], mx[:], None, ALU.is_ge))
        prog.phase()
        prog.on("vector", lambda e: e.scalar_tensor_tensor(t2[:], t2[:], -1e30, t1[:], ALU.mult, ALU.add))
        prog.phase()
        prog.on("vector", lambda e: e.reduce_max(mx[:], t2[:], axis=AX.X))
        prog.phase()
        prog.on("vector", lambda e: e.tensor_scalar(t2[:], t1[:], mx[:], None, ALU.is_ge))
        prog.phase()
        prog.on("vector", lambda e, _ti=ti: e.tensor_mul(c_sb[_ti][:], s_sb[_ti][:], t2[:]))
    prog.phase()
    for ti in range(16):
        prog.dma("gpsimd", comb[ti], c_sb[ti][:])
    return prog.build()


def stage_gate0(xn_fm, gate_w, gate_b):
    gwT = lhsT_tiles(np.ascontiguousarray(gate_w.T.astype(np.float32)))
    gbb = np.ascontiguousarray(np.tile(gate_b.astype(np.float32), (128, 1)))
    nc = get_prog("gate0", build_gate0)
    res = run8(nc, [{"x": xn_fm[c], "gw": gwT, "gb": gbb} for c in range(NCORES)])
    return [r["comb"].reshape(T, 8) for r in res]


def build_moe_expert():
    nc = bass.Bass("TRN2", target_bir_lowering=False)
    x = nc.dram_tensor("x", [8, 128, T], F32, kind="ExternalInput").ap()
    xmid = nc.dram_tensor("xmid", [8, 128, T], F32, kind="ExternalInput").ap()
    combd = nc.dram_tensor("combd", [128, T], F32, kind="ExternalInput").ap()
    w1g = nc.dram_tensor("w1g", [8, 128, 2048], F32, kind="ExternalInput").ap()
    w1l = nc.dram_tensor("w1l", [8, 128, 2048], F32, kind="ExternalInput").ap()
    w3g = nc.dram_tensor("w3g", [8, 128, 2048], F32, kind="ExternalInput").ap()
    w3l = nc.dram_tensor("w3l", [8, 128, 2048], F32, kind="ExternalInput").ap()
    w2 = nc.dram_tensor("w2", [16, 128, 1024], F32, kind="ExternalInput").ap()
    b1g = nc.dram_tensor("b1g", [128, 16], F32, kind="ExternalInput").ap()
    b1l = nc.dram_tensor("b1l", [128, 16], F32, kind="ExternalInput").ap()
    b3g = nc.dram_tensor("b3g", [128, 16], F32, kind="ExternalInput").ap()
    b3l = nc.dram_tensor("b3l", [128, 16], F32, kind="ExternalInput").ap()
    b2 = nc.dram_tensor("b2", [128, 8], F32, kind="ExternalInput").ap()
    out = nc.dram_tensor("out", [8, 128, T], F32, kind="ExternalOutput").ap()

    prog = Prog(nc)
    ctx = Ctx(nc)
    xt = [sb(nc, f"x{i}", 128, T) for i in range(8)]
    gacc = [sb(nc, f"ga{i}", 128, T) for i in range(8)]
    comb_sb = sb(nc, "compb", 128, T)
    wa = sb(nc, "wa", 128, 8 * 256)
    wb = sb(nc, "wb", 128, 8 * 256)
    w2s = sb(nc, "w2s", 128, 2 * 1024)
    G = [sb(nc, f"G{i}", 128, T) for i in range(2)]
    u1 = [sb(nc, f"u1_{i}", 128, 512) for i in range(2)]
    u2 = [sb(nc, f"u2_{i}", 128, 512) for i in range(2)]
    bb = {n: sb(nc, f"bb_{n}", 128, 16) for n in ("b1g", "b1l", "b3g", "b3l")}
    bb2 = sb(nc, "bb2", 128, 8)
    xm = [sb(nc, f"xm{i}", 128, T) for i in range(2)]

    for i in range(8):
        prog.dma("sync", xt[i][:], x[i])
        prog.on("gpsimd", lambda e, _i=i: e.memset(gacc[_i][:], 0.0))
    prog.dma("scalar", comb_sb[:], combd)
    for n, dr in (("b1g", b1g), ("b1l", b1l), ("b3g", b3g), ("b3l", b3l)):
        prog.dma("scalar", bb[n][:], dr)
    prog.dma("scalar", bb2[:], b2)

    TC = 512

    def uproj(wsb, bias_col, dest, n0):
        banks = [ctx.bank(), ctx.bank()]
        prog.phase()
        def mm(e, _w=wsb, _n0=n0, _bk=banks):
            ins = None
            for mi in range(2):
                for k in range(8):
                    ins = e.matmul(_bk[mi][0:128, 0:TC],
                                   _w[:, k * 256 + mi * 128:k * 256 + mi * 128 + 128],
                                   xt[k][:, _n0:_n0 + TC],
                                   start=(k == 0), stop=(k == 7))
            return ins
        prog.on("tensor", mm)
        prog.phase()
        for mi in range(2):
            prog.on("vector", lambda e, _mi=mi, _bk=banks, _bc=bias_col, _d=dest:
                    e.tensor_scalar_add(_d[_mi][:, 0:TC], _bk[_mi][0:128, 0:TC],
                                        bb[_bc[0]][:, _bc[1] * 2 + _mi:_bc[1] * 2 + _mi + 1]))

    for s in range(8):
        prog.phase()
        for k in range(8):
            prog.dma("sync", wa[:, k * 256:(k + 1) * 256], w1g[k][:, s * 256:(s + 1) * 256])
            prog.dma("gpsimd", wb[:, k * 256:(k + 1) * 256], w3g[k][:, s * 256:(s + 1) * 256])
        for k in range(2):
            prog.dma("scalar", w2s[:, k * 1024:(k + 1) * 1024], w2[s * 2 + k])
        for tc in range(4):
            n0 = tc * TC
            uproj(wa, ("b1g", s), u1, n0)
            uproj(wb, ("b3g", s), u2, n0)
            prog.phase()
            for mi in range(2):
                prog.on("vector", lambda e, _mi=mi: e.tensor_mul(u1[_mi][:, 0:TC], u1[_mi][:, 0:TC], u2[_mi][:, 0:TC]))
            prog.phase()
            for mi in range(2):
                prog.on("vector", lambda e, _mi=mi: e.tensor_scalar_min(u1[_mi][:, 0:TC], u1[_mi][:, 0:TC], 7.0))
            prog.phase()
            for mi in range(2):
                prog.on("scalar", lambda e, _mi=mi: e.activation(u2[_mi][:, 0:TC], u1[_mi][:, 0:TC], AF.Sigmoid, scale=1.702))
            prog.phase()
            for mi in range(2):
                prog.on("vector", lambda e, _mi=mi, _n0=n0: e.tensor_mul(
                    G[_mi][:, _n0:_n0 + TC], u1[_mi][:, 0:TC], u2[_mi][:, 0:TC]))
        prog.phase()
        for k in range(8):
            prog.dma("sync", wa[:, k * 256:(k + 1) * 256], w1l[k][:, s * 256:(s + 1) * 256])
            prog.dma("gpsimd", wb[:, k * 256:(k + 1) * 256], w3l[k][:, s * 256:(s + 1) * 256])
        for tc in range(4):
            n0 = tc * TC
            uproj(wa, ("b1l", s), u1, n0)
            uproj(wb, ("b3l", s), u2, n0)
            prog.phase()
            for mi in range(2):
                prog.on("vector", lambda e, _mi=mi: e.tensor_mul(u1[_mi][:, 0:TC], u1[_mi][:, 0:TC], u2[_mi][:, 0:TC]))
            prog.phase()
            for mi in range(2):
                prog.on("vector", lambda e, _mi=mi: e.tensor_scalar(u1[_mi][:, 0:TC], u1[_mi][:, 0:TC], 7.0, -7.0, ALU.min, ALU.max))
            prog.phase()
            for mi in range(2):
                prog.on("vector", lambda e, _mi=mi, _n0=n0: e.scalar_tensor_tensor(
                    u2[_mi][:, 0:TC], u1[_mi][:, 0:TC], 1.0, G[_mi][:, _n0:_n0 + TC],
                    ALU.add, ALU.mult))
            prog.phase()
            wk1 = [ctx.bank() for _ in range(4)]
            def mmw2a(e, _n0=n0, _bk=wk1):
                ins = None
                for m in range(4):
                    for k in range(2):
                        ins = e.matmul(_bk[m][0:128, 0:TC],
                                       w2s[:, k * 1024 + m * 128:k * 1024 + m * 128 + 128],
                                       u2[k][:, 0:TC], start=(k == 0), stop=(k == 1))
                return ins
            prog.on("tensor", mmw2a)
            prog.phase()
            wk2 = [ctx.bank() for _ in range(4)]
            def mmw2b(e, _n0=n0, _bk=wk2):
                ins = None
                for m in range(4, 8):
                    for k in range(2):
                        ins = e.matmul(_bk[m - 4][0:128, 0:TC],
                                       w2s[:, k * 1024 + m * 128:k * 1024 + m * 128 + 128],
                                       u2[k][:, 0:TC], start=(k == 0), stop=(k == 1))
                return ins
            prog.on("tensor", mmw2b)
            for m in range(4):
                prog.on("vector", lambda e, _m=m, _n0=n0, _bk=wk1: e.tensor_add(
                    gacc[_m][:, _n0:_n0 + TC], gacc[_m][:, _n0:_n0 + TC], _bk[_m][0:128, 0:TC]))
            prog.phase()
            for m in range(4, 8):
                prog.on("vector", lambda e, _m=m, _n0=n0, _bk=wk2: e.tensor_add(
                    gacc[_m][:, _n0:_n0 + TC], gacc[_m][:, _n0:_n0 + TC], _bk[_m - 4][0:128, 0:TC]))
    for m in range(8):
        prog.phase()
        prog.dma("sync", xm[m % 2][:], xmid[m])
        prog.phase()
        prog.on("vector", lambda e, _m=m: e.tensor_scalar_add(gacc[_m][:], gacc[_m][:], bb2[:, _m:_m + 1]))
        prog.phase()
        prog.on("vector", lambda e, _m=m: e.tensor_mul(gacc[_m][:], gacc[_m][:], comb_sb[:]))
        prog.phase()
        prog.on("vector", lambda e, _m=m: e.tensor_add(gacc[_m][:], gacc[_m][:], xm[_m % 2][:]))
        prog.phase()
        prog.dma("gpsimd", out[m], gacc[m][:])
    return prog.build()


def stage_moe_expert(xn_fm, xmid_fm, comb_cols, w1, b1, w3, b3, w2, b2):
    w1g = lhsT_tiles(np.ascontiguousarray(w1[:, 0::2].astype(np.float32)))
    w1l = lhsT_tiles(np.ascontiguousarray(w1[:, 1::2].astype(np.float32)))
    w3g = lhsT_tiles(np.ascontiguousarray(w3[:, 0::2].astype(np.float32)))
    w3l = lhsT_tiles(np.ascontiguousarray(w3[:, 1::2].astype(np.float32)))
    w2t = lhsT_tiles(np.ascontiguousarray(w2.astype(np.float32)))
    bt = {}
    for n, v, nn in (("b1g", b1[0::2], 16), ("b1l", b1[1::2], 16),
                     ("b3g", b3[0::2], 16), ("b3l", b3[1::2], 16)):
        bt[n] = np.ascontiguousarray(v.astype(np.float32).reshape(nn, 128).T)
    b2t = np.ascontiguousarray(b2.astype(np.float32).reshape(8, 128).T)
    nc = get_prog("moe_e", build_moe_expert)
    in_maps = []
    for c in range(NCORES):
        in_maps.append({
            "x": xn_fm[c], "xmid": xmid_fm[c],
            "combd": np.ascontiguousarray(np.tile(comb_cols[c], (128, 1))),
            "w1g": w1g, "w1l": w1l, "w3g": w3g, "w3l": w3l, "w2": w2t,
            "b1g": bt["b1g"], "b1l": bt["b1l"], "b3g": bt["b3g"], "b3l": bt["b3l"],
            "b2": b2t})
    res = run8(nc, in_maps)
    return [r["out"] for r in res]


def l0_ffn_apply(inp, xmid_fm, xn_fm):
    comb = stage_gate0(xn_fm, inp["L0_ffn_gate_w"], inp["L0_ffn_gate_b"])
    cur = xmid_fm
    for e in range(8):
        cc = [comb[c][:, e] for c in range(NCORES)]
        cur = stage_moe_expert(xn_fm, cur, cc,
                               inp["L0_ffn_w1"][e], inp["L0_ffn_b1"][e],
                               inp["L0_ffn_w3"][e], inp["L0_ffn_b3"][e],
                               inp["L0_ffn_w2"][e], inp["L0_ffn_b2"][e])
    ones = [np.ones(T, np.float32) for _ in range(NCORES)]
    cur = stage_moe_expert(xn_fm, cur, ones,
                           inp["L0_ffn_sw1"], inp["L0_ffn_sb1"],
                           inp["L0_ffn_sw3"], inp["L0_ffn_sb3"],
                           inp["L0_ffn_sw2"], inp["L0_ffn_sb2"])
    return cur


def build_add(ntiles):
    nc = bass.Bass("TRN2", target_bir_lowering=False)
    a = nc.dram_tensor("a", [ntiles, 128, T], F32, kind="ExternalInput").ap()
    b = nc.dram_tensor("b", [ntiles, 128, T], F32, kind="ExternalInput").ap()
    o = nc.dram_tensor("o", [ntiles, 128, T], F32, kind="ExternalOutput").ap()
    prog = Prog(nc)
    at = [sb(nc, f"a{i}", 128, T) for i in range(2)]
    bt = [sb(nc, f"b{i}", 128, T) for i in range(2)]
    for i in range(ntiles):
        prog.phase()
        prog.dma("sync", at[i % 2][:], a[i])
        prog.dma("scalar", bt[i % 2][:], b[i])
        prog.phase()
        prog.on("vector", lambda e, _i=i: e.tensor_add(at[_i % 2][:], at[_i % 2][:], bt[_i % 2][:]))
        prog.phase()
        prog.dma("gpsimd", o[i], at[i % 2][:])
    return prog.build()


def stage_add(a_fm, b_fm):
    nt = a_fm[0].shape[0]
    nc = get_prog("add", build_add, nt)
    res = run8(nc, [{"a": a_fm[c], "b": b_fm[c]} for c in range(NCORES)])
    return [r["o"] for r in res]


def build_l2mix():
    nc = bass.Bass("TRN2", target_bir_lowering=False)
    q = nc.dram_tensor("q", [8, 64, L], F32, kind="ExternalInput").ap()
    k = nc.dram_tensor("k", [8, 64, L], F32, kind="ExternalInput").ap()
    vT1 = nc.dram_tensor("vT1", [8, 128, 32 * 65], F32, kind="ExternalInput").ap()
    out = nc.dram_tensor("out", [8, 64, L], F32, kind="ExternalOutput").ap()
    prog = Prog(nc)
    ctx = Ctx(nc)
    q_sb = sb(nc, "q_s", 65, L)
    k_sb = sb(nc, "k_s", 65, L)
    vT1_sb = sb(nc, "v_s", 128, 32 * 65)
    att = [sb(nc, f"at{i}", 128, 3 * 512) for i in range(2)]
    setA = [ctx.reserve() for _ in range(3)]
    setB = [ctx.reserve() for _ in range(3)]
    obank = ctx.reserve()
    osb = sb(nc, "o_s", 64, L)
    den = sb(nc, "den", 1, 512)
    zb = sb(nc, "zb", 64, 512)
    ones64 = sb(nc, "o64", 1, 64)
    prog.on("gpsimd", lambda e: e.memset(ones64[:], 1.0))
    groups = [list(range(i, min(i + 3, 32))) for i in range(0, 32, 3)]
    ng = len(groups)
    for h in range(8):
        prog.phase()
        prog.dma("sync", q_sb[0:64, :], q[h])
        prog.dma("sync", k_sb[0:64, :], k[h])
        prog.dma("scalar", vT1_sb[:], vT1[h])
        prog.on("gpsimd", lambda e: e.memset(q_sb[64:65, :], 640.0))
        prog.on("gpsimd", lambda e: e.memset(k_sb[64:65, :], -1.0))
        for qc in range(8):
            n0 = qc * 512
            for idx in range(ng + 2):
                prog.phase()
                if idx < ng:
                    bset = setA if idx % 2 == 0 else setB
                    def mm_s(e, _g=groups[idx], _n0=n0, _bs=bset):
                        ins = None
                        for j, kt in enumerate(_g):
                            ins = e.matmul(_bs[j][0:128, 0:512],
                                           k_sb[:, kt * 128:(kt + 1) * 128],
                                           q_sb[:, _n0:_n0 + 512],
                                           start=True, stop=True)
                        return ins
                    prog.on("tensor", mm_s)
                if 0 <= idx - 1 < ng:
                    bset = setA if (idx - 1) % 2 == 0 else setB
                    for j in range(len(groups[idx - 1])):
                        prog.on("scalar", lambda e, _i=idx - 1, _j=j, _bs=bset: e.activation(
                            att[_i % 2][:, _j * 512:(_j + 1) * 512],
                            _bs[_j][0:128, 0:512], AF.Exp, scale=1.0 / 32.0))
                if 0 <= idx - 2 < ng:
                    def mm_av(e, _i=idx - 2, _g=groups[idx - 2], _ob=obank):
                        ins = None
                        for j, kt in enumerate(_g):
                            ins = e.matmul(_ob[0:65, 0:512],
                                           vT1_sb[:, kt * 65:(kt + 1) * 65],
                                           att[_i % 2][:, j * 512:(j + 1) * 512],
                                           start=(kt == 0), stop=(kt == 31))
                        return ins
                    prog.on("tensor", mm_av)
            prog.phase()
            prog.on("vector", lambda e, _ob=obank: e.tensor_copy(den[:], _ob[64:65, 0:512]))
            prog.phase()
            prog.on("vector", lambda e: e.reciprocal(den[:], den[:]))
            prog.phase()
            prog.on("tensor", lambda e: e.matmul(
                setA[0][0:64, 0:512], ones64[0:1, :], den[0:1, :], start=True, stop=True))
            prog.phase()
            prog.on("vector", lambda e: e.tensor_copy(zb[:], setA[0][0:64, 0:512]))
            prog.phase()
            prog.on("vector", lambda e, _ob=obank, _n0=n0: e.tensor_mul(
                osb[:, _n0:_n0 + 512], _ob[0:64, 0:512], zb[:]))
        prog.phase()
        prog.dma("gpsimd", out[h], osb[:])
        prog.phase()
    return prog.build()


def stage_l2mix(q_fm, k_fm, v_fm):
    qf = unshard_tokens([from_fm(p) for p in q_fm])
    kf = unshard_tokens([from_fm(p) for p in k_fm])
    vf = unshard_tokens([from_fm(p) for p in v_fm])
    in_maps = []
    for c in range(NCORES):
        m = {"q": [], "k": [], "vT1": []}
        for j in range(8):
            g = c * 8 + j
            b, h = g // 16, g % 16
            sl = slice(h * 64, (h + 1) * 64)
            m["q"].append(qf[b, :, sl].T)
            m["k"].append(kf[b, :, sl].T)
            vv = np.concatenate([vf[b, :, sl], np.ones((L, 1), np.float32)], 1)
            m["vT1"].append(vv.reshape(32, 128, 65).transpose(1, 0, 2).reshape(128, -1))
        in_maps.append({kk: np.ascontiguousarray(np.stack(vl)) for kk, vl in m.items()})
    nc = get_prog("l2mix", build_l2mix)
    res = run8(nc, in_maps)
    attn = np.zeros((B, L, D), np.float32)
    for c in range(NCORES):
        for j in range(8):
            g = c * 8 + j
            b, h = g // 16, g % 16
            attn[b, :, h * 64:(h + 1) * 64] = res[c]["out"][j].T
    return attn


def l2_attn_apply(inp, x_fm, xn_fm, q=None, k=None, v=None):
    if q is None:
        Wqkv = inp["L2_attn_W_qkv"]
        WT = np.ascontiguousarray(Wqkv.T)
        cols = np.arange(3072).reshape(16, 3, 64)
        q = stage_proj(xn_fm, np.ascontiguousarray(WT[:, cols[:, 0].reshape(-1)]))
        k = stage_proj(xn_fm, np.ascontiguousarray(WT[:, cols[:, 1].reshape(-1)]))
        v = stage_proj(xn_fm, np.ascontiguousarray(WT[:, cols[:, 2].reshape(-1)]))
    attn = stage_l2mix(q, k, v)
    attn_fm = [to_fm(p) for p in shard_tokens(attn)]
    WoT = np.ascontiguousarray(inp["L2_attn_W_out"].T)
    t = stage_proj(attn_fm, WoT, act="none", bias=inp["L2_attn_b_out"],
                   resid_parts_fm=x_fm, stream_out=True)
    return stage_add(t, xn_fm)


def build_gamma(ktiles, F):
    nc = bass.Bass("TRN2", target_bir_lowering=False)
    w = nc.dram_tensor("w", [ktiles, 128, F], F32, kind="ExternalInput").ap()
    gam = nc.dram_tensor("gam", [1, 1], F32, kind="ExternalOutput").ap()
    prog = Prog(nc)
    ctx = Ctx(nc)
    wt = [sb(nc, f"w{i}", 128, F) for i in range(2)]
    acc = sb(nc, "acc", 128, 1)
    r = sb(nc, "r", 128, 1)
    ones = sb(nc, "ones", 128, 1)
    g_sb = sb(nc, "g_sb", 1, 1)
    prog.on("gpsimd", lambda e: e.memset(acc[:], 0.0))
    prog.on("gpsimd", lambda e: e.memset(ones[:], 1.0))
    prog.dma("sync", wt[0][:], w[0])
    for k in range(ktiles):
        prog.phase()
        if k + 1 < ktiles:
            prog.dma("sync", wt[(k + 1) % 2][:], w[k + 1])
        prog.on("vector", lambda e, _k=k: e.tensor_reduce(
            r[:], wt[_k % 2][:], axis=AX.X, op=ALU.add, apply_absolute_value=True))
        prog.phase()
        prog.on("vector", lambda e: e.tensor_add(acc[:], acc[:], r[:]))
    prog.phase()
    bank = ctx.bank()
    prog.on("tensor", lambda e, _b=bank: e.matmul(_b[0:1, 0:1], acc[:, 0:1],
                                                  ones[:, 0:1], start=True, stop=True))
    prog.phase()
    prog.on("scalar", lambda e, _b=bank: e.activation(
        g_sb[:], _b[0:1, 0:1], AF.Copy, scale=1.0 / (ktiles * 128 * F)))
    prog.phase()
    prog.on("vector", lambda e: e.tensor_scalar_add(g_sb[:], g_sb[:], 1e-6))
    prog.phase()
    prog.dma("gpsimd", gam, g_sb[:])
    return prog.build()


def build_quant(ktiles, F):
    nc = bass.Bass("TRN2", target_bir_lowering=False)
    w = nc.dram_tensor("w", [ktiles, 128, F], F32, kind="ExternalInput").ap()
    gamd = nc.dram_tensor("gam", [1, 1], F32, kind="ExternalInput").ap()
    wq = nc.dram_tensor("wq", [ktiles, 128, F], F32, kind="ExternalOutput").ap()
    prog = Prog(nc)
    ctx = Ctx(nc)
    wt = [sb(nc, f"w{i}", 128, F) for i in range(2)]
    at = [sb(nc, f"a{i}", 128, F) for i in range(2)]
    st_ = [sb(nc, f"s{i}", 128, F) for i in range(2)]
    g_sb = sb(nc, "g_sb", 1, 1)
    onesr = sb(nc, "onesr", 1, 128)
    th = sb(nc, "th", 128, 1)
    prog.dma("sync", g_sb[:], gamd)
    prog.on("gpsimd", lambda e: e.memset(onesr[:], 1.0))
    prog.phase()
    bank = ctx.bank()
    prog.on("tensor", lambda e, _b=bank: e.matmul(_b[0:128, 0:1], onesr[0:1, :],
                                                  g_sb[0:1, 0:1], start=True, stop=True))
    prog.phase()
    prog.on("scalar", lambda e, _b=bank: e.activation(th[:], _b[0:128, 0:1], AF.Copy, scale=0.5))
    prog.dma("sync", wt[0][:], w[0])
    for k in range(ktiles):
        prog.phase()
        if k + 1 < ktiles:
            prog.dma("sync", wt[(k + 1) % 2][:], w[k + 1])
        prog.on("scalar", lambda e, _k=k: e.activation(at[_k % 2][:], wt[_k % 2][:], AF.Abs))
        prog.on("scalar", lambda e, _k=k: e.sign(st_[_k % 2][:], wt[_k % 2][:]))
        prog.phase()
        prog.on("vector", lambda e, _k=k: e.tensor_scalar(
            at[_k % 2][:], at[_k % 2][:], th[:, 0:1], None, ALU.is_gt))
        prog.phase()
        prog.on("vector", lambda e, _k=k: e.tensor_mul(st_[_k % 2][:], at[_k % 2][:], st_[_k % 2][:]))
        prog.phase()
        prog.dma("gpsimd", wq[k], st_[k % 2][:])
    return prog.build()


def build_lnbig(ktiles):
    nc = bass.Bass("TRN2", target_bir_lowering=False)
    F = ktiles * 128
    h = nc.dram_tensor("h", [ktiles, 128, T], F32, kind="ExternalInput").ap()
    wg = nc.dram_tensor("wg", [ktiles, 128, 1], F32, kind="ExternalInput").ap()
    wb = nc.dram_tensor("wb", [ktiles, 128, 1], F32, kind="ExternalInput").ap()
    out = nc.dram_tensor("out", [ktiles, 128, T], F32, kind="ExternalOutput").ap()
    prog = Prog(nc)
    ctx = Ctx(nc)
    ht = [sb(nc, f"h{i}", 128, T) for i in range(2)]
    scr = [sb(nc, f"scr{i}", 128, T) for i in range(2)]
    wgt = sb(nc, "wgt", 128, ktiles)
    wbt = sb(nc, "wbt", 128, ktiles)
    ones1 = sb(nc, "ones1", 128, 1)
    onesr = sb(nc, "onesr", 1, 128)
    mean = sb(nc, "mean", 1, T)
    exsq = sb(nc, "exsq", 1, T)
    m2 = sb(nc, "m2", 1, T)
    var = sb(nc, "var", 1, T)
    mb = sb(nc, "mb", 128, T)
    rb = sb(nc, "rb", 128, T)
    prog.on("gpsimd", lambda e: e.memset(ones1[:], 1.0))
    prog.on("gpsimd", lambda e: e.memset(onesr[:], 1.0))
    for i in range(ktiles):
        prog.dma("scalar", wgt[:, i:i + 1], wg[i])
        prog.dma("scalar", wbt[:, i:i + 1], wb[i])
    prog.dma("sync", ht[0][:], h[0])
    sbank = [ctx.reserve() for _ in range(4)]
    qbank = [ctx.reserve() for _ in range(4)]
    for k in range(ktiles):
        prog.phase()
        if k + 1 < ktiles:
            prog.dma("sync", ht[(k + 1) % 2][:], h[k + 1])
        prog.on("scalar", lambda e, _k=k: e.activation(scr[_k % 2][:], ht[_k % 2][:], AF.Square))
        def mmsum(e, _k=k):
            ins = None
            for c in range(4):
                ins = e.matmul(sbank[c][0:1, :], ones1[:, 0:1],
                               ht[_k % 2][:, c * 512:(c + 1) * 512],
                               start=(_k == 0), stop=(_k == ktiles - 1))
            return ins
        prog.on("tensor", mmsum)
        prog.phase()
        def mmsq(e, _k=k):
            ins = None
            for c in range(4):
                ins = e.matmul(qbank[c][0:1, :], ones1[:, 0:1],
                               scr[_k % 2][:, c * 512:(c + 1) * 512],
                               start=(_k == 0), stop=(_k == ktiles - 1))
            return ins
        prog.on("tensor", mmsq)
    prog.phase()
    for c in range(4):
        prog.on("scalar", lambda e, _c=c: e.activation(
            mean[:, _c * 512:(_c + 1) * 512], sbank[_c][0:1, :], AF.Copy, scale=1.0 / F))
        prog.on("vector", lambda e, _c=c: e.tensor_scalar_mul(
            exsq[:, _c * 512:(_c + 1) * 512], qbank[_c][0:1, :], 1.0 / F))
    prog.phase()
    prog.on("vector", lambda e: e.tensor_mul(m2[:], mean[:], mean[:]))
    prog.phase()
    prog.on("vector", lambda e: e.tensor_sub(var[:], exsq[:], m2[:]))
    prog.phase()
    prog.on("vector", lambda e: e.tensor_scalar_add(var[:], var[:], 1e-5))
    prog.phase()
    prog.on("scalar", lambda e: e.activation(var[:], var[:], AF.Sqrt))
    prog.phase()
    prog.on("vector", lambda e: e.reciprocal(var[:], var[:]))
    prog.phase()
    def mmb(e):
        ins = None
        for c in range(4):
            ins = e.matmul(sbank[c][:, :], onesr[0:1, :],
                           mean[:, c * 512:(c + 1) * 512], start=True, stop=True)
            ins = e.matmul(qbank[c][:, :], onesr[0:1, :],
                           var[:, c * 512:(c + 1) * 512], start=True, stop=True)
        return ins
    prog.on("tensor", mmb)
    prog.phase()
    for c in range(4):
        prog.on("scalar", lambda e, _c=c: e.activation(
            mb[:, _c * 512:(_c + 1) * 512], sbank[_c][:, :], AF.Copy))
        prog.on("vector", lambda e, _c=c: e.tensor_copy(
            rb[:, _c * 512:(_c + 1) * 512], qbank[_c][:, :]))
    prog.phase()
    prog.dma("sync", ht[0][:], h[0])
    for k in range(ktiles):
        prog.phase()
        if k + 1 < ktiles:
            prog.dma("sync", ht[(k + 1) % 2][:], h[k + 1])
        prog.on("vector", lambda e, _k=k: e.tensor_sub(ht[_k % 2][:], ht[_k % 2][:], mb[:]))
        prog.phase()
        prog.on("vector", lambda e, _k=k: e.tensor_mul(ht[_k % 2][:], ht[_k % 2][:], rb[:]))
        prog.phase()
        prog.on("vector", lambda e, _k=k: e.tensor_scalar(
            ht[_k % 2][:], ht[_k % 2][:], wgt[:, _k:_k + 1], wbt[:, _k:_k + 1],
            ALU.mult, ALU.add))
        prog.phase()
        prog.dma("gpsimd", out[k], ht[k % 2][:])
    return prog.build()


def build_bigkproj(ktiles, mout, has_bias):
    nc = bass.Bass("TRN2", target_bir_lowering=False)
    mtiles = (mout + 127) // 128
    x = nc.dram_tensor("x", [ktiles, 128, T], F32, kind="ExternalInput").ap()
    w = nc.dram_tensor("w", [ktiles, 128, mout], F32, kind="ExternalInput").ap()
    r = nc.dram_tensor("r", [mtiles, 128, T], F32, kind="ExternalInput").ap()
    if has_bias:
        bdr = nc.dram_tensor("b", [mtiles, 128, 1], F32, kind="ExternalInput").ap()
    out = nc.dram_tensor("out", [mtiles, 128, T], F32, kind="ExternalOutput").ap()
    prog = Prog(nc)
    ctx = Ctx(nc)
    rt = [sb(nc, f"r{i}", 128, T) for i in range(mtiles)]
    xt = [sb(nc, f"x{i}", 128, 512) for i in range(2)]
    wt = [sb(nc, f"w{i}", 128, mout) for i in range(2)]
    bt = [sb(nc, f"b{i}", 128, 1) for i in range(mtiles)] if has_bias else None
    for i in range(mtiles):
        prog.dma("scalar", rt[i][:], r[i])
        if has_bias:
            prog.dma("scalar", bt[i][:], bdr[i])
    banks = [ctx.reserve() for _ in range(8)]
    for n in range(4):
        n0 = n * 512
        prog.phase()
        prog.dma("sync", xt[0][:], x[0][:, n0:n0 + 512])
        prog.dma("gpsimd", wt[0][:], w[0])
        for k in range(ktiles):
            prog.phase()
            if k + 1 < ktiles:
                prog.dma("sync", xt[(k + 1) % 2][:], x[k + 1][:, n0:n0 + 512])
                prog.dma("gpsimd", wt[(k + 1) % 2][:], w[k + 1])
            def mm(e, _k=k):
                ins = None
                for m in range(mtiles):
                    mw = min(128, mout - m * 128)
                    ins = e.matmul(banks[m][0:mw, 0:512],
                                   wt[_k % 2][:, m * 128:m * 128 + mw],
                                   xt[_k % 2][:, 0:512],
                                   start=(_k == 0), stop=(_k == ktiles - 1))
                return ins
            prog.on("tensor", mm)
        prog.phase()
        for m in range(mtiles):
            mw = min(128, mout - m * 128)
            if has_bias:
                prog.on("vector", lambda e, _m=m, _mw=mw, _n0=n0: e.scalar_tensor_tensor(
                    rt[_m][0:_mw, _n0:_n0 + 512], banks[_m][0:_mw, 0:512],
                    bt[_m][0:_mw, :], rt[_m][0:_mw, _n0:_n0 + 512], ALU.add, ALU.add))
            else:
                prog.on("vector", lambda e, _m=m, _mw=mw, _n0=n0: e.tensor_add(
                    rt[_m][0:_mw, _n0:_n0 + 512], rt[_m][0:_mw, _n0:_n0 + 512],
                    banks[_m][0:_mw, 0:512]))
    prog.phase()
    for m in range(mtiles):
        prog.dma("gpsimd", out[m], rt[m][:])
    return prog.build()


def stage_gamma(Wt):
    kt, _, F = Wt.shape
    nc = get_prog("gamma", build_gamma, kt, F)
    res = run8(nc, [{"w": Wt} for _ in range(NCORES)])
    return [r["gam"] for r in res]


def stage_quant(Wt, gams):
    kt, _, F = Wt.shape
    nc = get_prog("quant", build_quant, kt, F)
    res = run8(nc, [{"w": Wt, "gam": gams[c]} for c in range(NCORES)])
    return [r["wq"] for r in res]


def stage_lnbig(h_parts, wg, wbias):
    kt = h_parts[0].shape[0]
    wgt = np.ascontiguousarray(wg.astype(np.float32).reshape(kt, 128, 1))
    wbt = np.ascontiguousarray(wbias.astype(np.float32).reshape(kt, 128, 1))
    nc = get_prog("lnbig", build_lnbig, kt)
    res = run8(nc, [{"h": h_parts[c], "wg": wgt, "wb": wbt} for c in range(NCORES)])
    return [r["out"] for r in res]


def stage_bigkproj(x_parts, w_tiles_percore, resid_parts, bias=None):
    kt = x_parts[0].shape[0]
    mout = w_tiles_percore[0].shape[2]
    mtiles = (mout + 127) // 128
    nc = get_prog("bigk", build_bigkproj, kt, mout, bias is not None)
    in_maps = []
    for c in range(NCORES):
        m = {"x": x_parts[c], "w": w_tiles_percore[c], "r": resid_parts[c]}
        if bias is not None:
            bp = np.zeros(mtiles * 128, np.float32)
            bp[:mout] = bias
            m["b"] = np.ascontiguousarray(bp.reshape(mtiles, 128, 1))
        in_maps.append(m)
    res = run8(nc, in_maps)
    return [r["out"] for r in res]


def l2_ffn_apply(inp, xacc_fm, xn_fm):
    WfcT = lhsT_tiles(np.ascontiguousarray(inp["L2_ffn_W_fc"].T.astype(np.float32)))
    WprT = lhsT_tiles(np.ascontiguousarray(inp["L2_ffn_W_proj"].T.astype(np.float32)))
    ncq = get_prog("gq", build_gq)
    resq = run8(ncq, [{"gqw1": WfcT, "gqw2": WprT} for _ in range(NCORES)])
    wq1 = [r["q1"] for r in resq]
    wq2 = [r["q2"] for r in resq]
    h = stage_proj_pertile(xn_fm, wq1, 4096, act="gelu", stream_out=True,
                           stream_w=True)
    hn = stage_lnbig(h, inp["L2_ffn_ln_w"], inp["L2_ffn_ln_b"])
    return stage_bigkproj(hn, wq2, xacc_fm)


def stage_proj_pertile(x_parts_fm, w_tiles_percore, mout, act="none",
                       stream_out=False, stream_w=False):
    kin = w_tiles_percore[0].shape[0] * 128
    Tloc = x_parts_fm[0].shape[2]
    nc = get_prog("proj", build_proj, kin, mout, act, False, False, stream_out,
                  Tloc, stream_w)
    res = run8(nc, [{"x": x_parts_fm[c], "w": w_tiles_percore[c]}
                    for c in range(NCORES)])
    return [r["out"] for r in res]


def build_l3scanprep():
    nc = bass.Bass("TRN2", target_bir_lowering=False)
    TO = T + 128
    xp = nc.dram_tensor("xp", [16, 128, TO], F32, kind="ExternalInput").ap()
    cw = nc.dram_tensor("cw", [16, 128, 4], F32, kind="ExternalInput").ap()
    cb = nc.dram_tensor("cb", [16, 128, 1], F32, kind="ExternalInput").ap()
    wB = nc.dram_tensor("wB", [16, 128, 16], F32, kind="ExternalInput").ap()
    U = nc.dram_tensor("U", [16, T], F32, kind="ExternalOutput").ap()
    prog = Prog(nc)
    ctx = Ctx(nc)
    xpt = [sb(nc, f"xp{i}", 128, TO) for i in range(2)]
    cwt = sb(nc, "cwt", 128, 64)
    cbt = sb(nc, "cbt", 128, 16)
    wBt = sb(nc, "wBt", 128, 16 * 16)
    acc = sb(nc, "accx", 128, T)
    xc = sb(nc, "xc", 128, T)
    u_sb = sb(nc, "u_sb", 16, T)
    ubank = [ctx.reserve() for _ in range(4)]
    for i in range(16):
        prog.dma("scalar", cwt[:, i * 4:(i + 1) * 4], cw[i])
        prog.dma("scalar", cbt[:, i:i + 1], cb[i])
        prog.dma("scalar", wBt[:, i * 16:(i + 1) * 16], wB[i])
    prog.dma("sync", xpt[0][:], xp[0])
    for i in range(16):
        prog.phase()
        if i + 1 < 16:
            prog.dma("sync", xpt[(i + 1) % 2][:], xp[i + 1])
        prog.on("vector", lambda e, _i=i: e.tensor_scalar(
            acc[:], xpt[_i % 2][:, 125:125 + T], cwt[:, _i * 4:_i * 4 + 1], None, ALU.mult))
        for j in (1, 2, 3):
            prog.phase()
            prog.on("vector", lambda e, _i=i, _j=j: e.scalar_tensor_tensor(
                acc[:], xpt[_i % 2][:, 125 + _j:125 + _j + T],
                cwt[:, _i * 4 + _j:_i * 4 + _j + 1], acc[:], ALU.mult, ALU.add))
        prog.phase()
        prog.on("scalar", lambda e, _i=i: e.activation(
            xc[:], acc[:], AF.Silu, bias=cbt[:, _i:_i + 1]))
        prog.phase()
        def mmu(e, _i=i):
            ins = None
            for c in range(4):
                ins = e.matmul(ubank[c][0:16, 0:512], wBt[:, _i * 16:(_i + 1) * 16],
                               xc[:, c * 512:(c + 1) * 512],
                               start=(_i == 0), stop=(_i == 15))
            return ins
        prog.on("tensor", mmu)
    prog.phase()
    for c in range(4):
        prog.on("scalar", lambda e, _c=c: e.activation(
            u_sb[:, _c * 512:(_c + 1) * 512], ubank[_c][0:16, 0:512], AF.Copy, scale=0.1))
    prog.phase()
    prog.dma("gpsimd", U, u_sb[:])
    return prog.build()


def build_l3scan():
    nc = bass.Bass("TRN2", target_bir_lowering=False)
    Ud = nc.dram_tensor("U", [16, L], F32, kind="ExternalInput").ap()
    zd = nc.dram_tensor("z", [16, 128, T], F32, kind="ExternalInput").ap()
    wC = nc.dram_tensor("wC", [16, 2048], F32, kind="ExternalInput").ap()
    wO = nc.dram_tensor("wO", [16, 128, 1024], F32, kind="ExternalInput").ap()
    r = nc.dram_tensor("r", [8, 128, T], F32, kind="ExternalInput").ap()
    mab = nc.dram_tensor("mab", [16, 2], F32, kind="ExternalInput").ap()
    out = nc.dram_tensor("out", [8, 128, T], F32, kind="ExternalOutput").ap()
    prog = Prog(nc)
    ctx = Ctx(nc)
    u_sb = sb(nc, "u_sb", 16, L)
    dec = sb(nc, "dec", 16, T)
    H = sb(nc, "H", 16, L)
    Hown = sb(nc, "Hown", 16, T)
    zt = [sb(nc, f"z{i}", 128, T) for i in range(4)]
    wCt = sb(nc, "wCt", 16, 2048)
    wOt = [sb(nc, f"wO{i}", 128, 1024) for i in range(4)]
    rt = [sb(nc, f"r{i}", 128, T) for i in range(8)]
    ys = [sb(nc, f"ys{i}", 128, T) for i in range(4)]
    mabt = sb(nc, "mabt", 16, 2)
    prog.dma("sync", u_sb[:], Ud)
    prog.dma("sync", wCt[:], wC)
    prog.dma("sync", mabt[:], mab)
    for i in range(8):
        prog.dma("gpsimd", rt[i][:], r[i])
    prog.on("vector", lambda e: e.memset(dec[:], 0.9))
    prog.phase()
    prog.on("vector", lambda e: e.tensor_tensor_scan(H[:, 0:T], dec[:], u_sb[:, 0:T],
                                                     0.0, ALU.mult, ALU.add))
    prog.phase()
    prog.on("vector", lambda e: e.tensor_tensor_scan(H[:, T:2 * T], dec[:],
                                                     u_sb[:, T:2 * T],
                                                     H[:, T - 1:T], ALU.mult, ALU.add))
    prog.phase()
    prog.on("vector", lambda e: e.tensor_scalar(Hown[:], H[:, 0:T], mabt[:, 0:1], None, ALU.mult))
    prog.phase()
    prog.on("vector", lambda e: e.scalar_tensor_tensor(Hown[:], H[:, T:2 * T], mabt[:, 1:2], Hown[:], ALU.mult, ALU.add))
    for ms in range(4):
        prog.phase()
        for k in range(4):
            prog.dma("sync", wOt[k][:], wO[ms * 4 + k])
            prog.dma("scalar", zt[k][:], zd[ms * 4 + k])
        prog.phase()
        for k in range(4):
            prog.on("scalar", lambda e, _k=k: e.activation(zt[_k][:], zt[_k][:], AF.Silu))
        for rnd in range(2):
            prog.phase()
            bks = [ctx.bank() for _ in range(8)]
            def mmys(e, _ms=ms, _rnd=rnd, _bks=bks):
                ins = None
                for idx in range(8):
                    mi, n = (idx // 4 + 2 * _rnd), idx % 4
                    ins = e.matmul(_bks[idx][0:128, 0:512],
                                   wCt[0:16, _ms * 512 + mi * 128:_ms * 512 + mi * 128 + 128],
                                   Hown[0:16, n * 512:(n + 1) * 512],
                                   start=True, stop=True)
                return ins
            prog.on("tensor", mmys)
            prog.phase()
            for idx in range(8):
                mi, n = (idx // 4 + 2 * rnd), idx % 4
                prog.on("vector", lambda e, _mi=mi, _n=n, _b=bks[idx]: e.tensor_mul(
                    ys[_mi][:, _n * 512:(_n + 1) * 512], _b[0:128, 0:512],
                    zt[_mi][:, _n * 512:(_n + 1) * 512]))
        for n in range(4):
            prog.phase()
            obks = [ctx.bank() for _ in range(8)]
            def mmo(e, _n=n, _bks=obks):
                ins = None
                for m in range(8):
                    for k in range(4):
                        ins = e.matmul(_bks[m][0:128, 0:512],
                                       wOt[k][:, m * 128:(m + 1) * 128],
                                       ys[k][:, _n * 512:(_n + 1) * 512],
                                       start=(k == 0), stop=(k == 3))
                return ins
            prog.on("tensor", mmo)
            prog.phase()
            for m in range(8):
                prog.on("vector", lambda e, _m=m, _n=n, _b=obks[_m] if False else obks[m]: e.tensor_add(
                    rt[_m][:, _n * 512:(_n + 1) * 512], rt[_m][:, _n * 512:(_n + 1) * 512],
                    _b[0:128, 0:512]))
    prog.phase()
    for m in range(8):
        prog.dma("gpsimd", out[m], rt[m][:])
    return prog.build()


def l3_attn_apply(inp, x_fm, xn_fm, xp=None, z=None):
    w_in = inp["L3_attn_w_in"]
    if xp is None:
        xp = stage_proj(xn_fm, np.ascontiguousarray(w_in[:, :2048]), stream_w=True)
        z = stage_proj(xn_fm, np.ascontiguousarray(w_in[:, 2048:]), stream_w=True)
    xp_full = unshard_tokens([from_fm(p) for p in xp])
    in_maps = []
    cw = np.ascontiguousarray(inp["L3_attn_conv_w"][:, 0, :].astype(np.float32)
                              .reshape(16, 128, 4))
    cb = np.ascontiguousarray(inp["L3_attn_conv_b"].astype(np.float32).reshape(16, 128, 1))
    wB = np.ascontiguousarray(inp["L3_attn_w_B"].astype(np.float32).reshape(16, 128, 16))
    for c in range(NCORES):
        b, hf = c // 2, c % 2
        blk = np.zeros((T + 128, 2048), np.float32)
        blk[128:] = xp_full[b, hf * T:(hf + 1) * T]
        if hf == 1:
            blk[:128] = xp_full[b, T - 128:T]
        in_maps.append({"xp": to_fm(blk), "cw": cw, "cb": cb, "wB": wB})
    nc = get_prog("l3prep", build_l3scanprep)
    res = run8(nc, in_maps)
    Ufull = np.zeros((B, 16, L), np.float32)
    for c in range(NCORES):
        b, hf = c // 2, c % 2
        Ufull[b, :, hf * T:(hf + 1) * T] = res[c]["U"]
    wCt = np.ascontiguousarray(inp["L3_attn_w_C"].astype(np.float32))
    wOt = lhsT_tiles(np.ascontiguousarray(inp["L3_attn_w_out"].astype(np.float32)))
    in_maps2 = []
    for c in range(NCORES):
        b, hf = c // 2, c % 2
        mab = np.zeros((16, 2), np.float32)
        mab[:, hf] = 1.0
        in_maps2.append({"U": Ufull[b], "z": z[c], "wC": wCt, "wO": wOt,
                         "r": x_fm[c], "mab": mab})
    nc2 = get_prog("l3scan", build_l3scan)
    res2 = run8(nc2, in_maps2)
    return [r_["out"] for r_ in res2]


def build_tree():
    nc = bass.Bass("TRN2", target_bir_lowering=False)
    WID = 2040
    x = nc.dram_tensor("x", [8, 128, T], F32, kind="ExternalInput").ap()
    w = nc.dram_tensor("w", [8, 128, WID], F32, kind="ExternalInput").ap()
    bbd = nc.dram_tensor("bb", [128, WID], F32, kind="ExternalInput").ap()
    out = nc.dram_tensor("out", [16, 128, WID], F32, kind="ExternalOutput").ap()
    prog = Prog(nc)
    ctx = Ctx(nc)
    xt = [sb(nc, f"x{i}", 128, T) for i in range(8)]
    wt = [sb(nc, f"w{i}", 128, WID) for i in range(8)]
    bbt = sb(nc, "bbt", 128, WID)
    lg = sb(nc, "lg", 128, WID)
    ac = sb(nc, "ac", 128, WID)
    dc = sb(nc, "dc", 128, WID)
    sel = sb(nc, "sel", 128, 8 * 255)
    for i in range(8):
        prog.dma("sync", xt[i][:], x[i])
        prog.dma("scalar", wt[i][:], w[i])
    prog.dma("sync", bbt[:], bbd)
    NSP = [512, 512, 512, 504]
    for ti in range(16):
        prog.phase()
        banks = [ctx.bank() for _ in range(4)]
        def mml(e, _ti=ti, _bks=banks):
            ins = None
            for c in range(4):
                n0 = c * 512
                nw = NSP[c]
                for k in range(8):
                    ins = e.matmul(_bks[c][0:128, 0:nw],
                                   xt[k][:, _ti * 128:(_ti + 1) * 128],
                                   wt[k][:, n0:n0 + nw],
                                   start=(k == 0), stop=(k == 7))
            return ins
        prog.on("tensor", mml)
        prog.phase()
        for c in range(4):
            n0 = c * 512
            nw = NSP[c]
            prog.on("vector", lambda e, _n0=n0, _nw=nw, _b=banks[c]: e.tensor_add(
                lg[:, _n0:_n0 + _nw], _b[0:128, 0:_nw], bbt[:, _n0:_n0 + _nw]))
        prog.phase()
        prog.on("scalar", lambda e: e.activation(ac[:], lg[:], AF.Silu))
        prog.on("vector", lambda e: e.tensor_single_scalar(dc[:], lg[:], 0.0, ALU.is_gt))
        prog.on("gpsimd", lambda e: e.memset(sel[:], 0.0))
        prog.phase()
        sel3 = sel[:].rearrange("p (t n) -> p t n", n=255)
        dc3 = dc[:].rearrange("p (t n) -> p t n", n=255)
        prog.on("gpsimd", lambda e, _s=sel3: e.memset(_s[:, :, 0:1], 1.0))
        for d in range(7):
            base = 2 ** d - 1
            wdt = 2 ** d
            nxt = 2 ** (d + 1) - 1
            prog.phase()
            prog.on("vector", lambda e, _b=base, _w=wdt, _n=nxt, _s=sel3, _d=dc3: e.tensor_mul(
                _s[:, :, _n + 1:_n + 2 * _w:2], _s[:, :, _b:_b + _w], _d[:, :, _b:_b + _w]))
            prog.phase()
            prog.on("vector", lambda e, _b=base, _w=wdt, _n=nxt, _s=sel3: e.tensor_sub(
                _s[:, :, _n:_n + 2 * _w:2], _s[:, :, _b:_b + _w], _s[:, :, _n + 1:_n + 2 * _w:2]))
        prog.phase()
        prog.on("vector", lambda e: e.tensor_mul(ac[:], ac[:], sel[:]))
        prog.phase()
        prog.dma("gpsimd", out[ti], ac[:])
        prog.phase()
    return prog.build()


def l3_ffn_apply(inp, xacc_fm, xn_fm):
    WinT = lhsT_tiles(np.ascontiguousarray(inp["L3_ffn_W_in"].T.astype(np.float32)))
    binb = np.ascontiguousarray(np.tile(inp["L3_ffn_b_in"].astype(np.float32), (128, 1)))
    nc = get_prog("tree", build_tree)
    res = run8(nc, [{"x": xn_fm[c], "w": WinT, "bb": binb} for c in range(NCORES)])
    WoutT = np.ascontiguousarray(inp["L3_ffn_W_out"].T.astype(np.float32))
    WoutTp = lhsT_tiles(WoutT)
    masked_fm = []
    for c in range(NCORES):
        mtm = res[c]["out"].reshape(T, 2040)
        masked_fm.append(to_fm(pad_feat(mtm)))
    return stage_bigkproj(masked_fm, [WoutTp] * NCORES, xacc_fm)


TOV = 2688
NTI = TOV // 128


def build_l1qkv():
    nc = bass.Bass("TRN2", target_bir_lowering=False)
    x = nc.dram_tensor("x", [8, 128, TOV], F32, kind="ExternalInput").ap()
    wqkv = nc.dram_tensor("wqkv", [8, 128, 1536], F32, kind="ExternalInput").ap()
    qsc = nc.dram_tensor("qsc", [128, 512], F32, kind="ExternalInput").ap()
    ksc = nc.dram_tensor("ksc", [128, 512], F32, kind="ExternalInput").ap()
    csq = nc.dram_tensor("csq", [NTI, 128, 512], F32, kind="ExternalInput").ap()
    ssq = nc.dram_tensor("ssq", [NTI, 128, 512], F32, kind="ExternalInput").ap()
    csk = nc.dram_tensor("csk", [NTI, 128, 512], F32, kind="ExternalInput").ap()
    ssk = nc.dram_tensor("ssk", [NTI, 128, 512], F32, kind="ExternalInput").ap()
    idn = nc.dram_tensor("idn", [128, 128], F32, kind="ExternalInput").ap()
    qf = nc.dram_tensor("qf", [4, 128, TOV], F32, kind="ExternalOutput").ap()
    kf = nc.dram_tensor("kf", [4, 128, TOV], F32, kind="ExternalOutput").ap()
    vt = nc.dram_tensor("vt", [NTI, 128, 520], F32, kind="ExternalOutput").ap()
    prog = Prog(nc)
    ctx = Ctx(nc)
    xt = [sb(nc, f"x{i}", 128, TOV) for i in range(8)]
    h0 = [sb(nc, f"h0_{i}", 128, 128) for i in range(4)]
    wt = [sb(nc, f"w{i}", 128, 1536) for i in range(8)]
    ident = sb(nc, "idn_s", 128, 128)
    qsct = sb(nc, "qsct", 128, 512)
    ksct = sb(nc, "ksct", 128, 512)
    cq = sb(nc, "cq", 128, 512)
    sq = sb(nc, "sq", 128, 512)
    ck = sb(nc, "ck", 128, 512)
    sk = sb(nc, "sk", 128, 512)
    qkvt = sb(nc, "qkvt", 128, 1536)
    sqv = sb(nc, "sqv", 128, 512)
    nrm = sb(nc, "nrm", 128, 8)
    tq = sb(nc, "tq", 128, 512)
    tk = sb(nc, "tk", 128, 512)
    ta = sb(nc, "ta", 128, 512)
    tb = sb(nc, "tb", 128, 512)
    vt1 = sb(nc, "vt1", 128, 520)
    scr = sb(nc, "scr", 128, 128)

    for i in range(8):
        prog.dma("sync", xt[i][:], x[i])
        prog.dma("scalar", wt[i][:], wqkv[i])
    prog.dma("scalar", qsct[:], qsc)
    prog.dma("scalar", ksct[:], ksc)
    prog.dma("gpsimd", ident[:], idn)
    prog.phase()
    for i in range(4):
        prog.on("gpsimd", lambda e, _i=i: e.memset(h0[_i][:, 0:1], 0.0))
        prog.on("vector", lambda e, _i=i: e.tensor_copy(h0[_i][:, 1:128], xt[4 + _i][:, 0:127]))

    def l2norm_rotary(base, sct, dest, cosb, sinb):
        prog.phase()
        prog.on("scalar", lambda e: e.activation(sqv[:], qkvt[:, base:base + 512], AF.Square))
        prog.phase()
        for h in range(8):
            prog.on("vector", lambda e, _h=h: e.tensor_reduce(
                nrm[:, _h:_h + 1], sqv[:, _h * 64:(_h + 1) * 64], axis=AX.X, op=ALU.add))
        prog.phase()
        prog.on("vector", lambda e: e.tensor_scalar_add(nrm[:], nrm[:], 1e-12))
        prog.phase()
        prog.on("scalar", lambda e: e.activation(nrm[:], nrm[:], AF.Sqrt))
        prog.phase()
        prog.on("vector", lambda e: e.reciprocal(nrm[:], nrm[:]))
        prog.phase()
        for h in range(8):
            prog.on("vector", lambda e, _h=h: e.tensor_scalar(
                ta[:, _h * 64:(_h + 1) * 64], qkvt[:, base + _h * 64:base + (_h + 1) * 64],
                nrm[:, _h:_h + 1], None, ALU.mult))
        prog.phase()
        prog.on("vector", lambda e: e.tensor_mul(ta[:], ta[:], sct[:]))
        prog.phase()
        ta3 = ta[:].rearrange("p (h t d) -> p h t d", h=8, t=2)
        tb3 = tb[:].rearrange("p (h t d) -> p h t d", h=8, t=2)
        sb3 = sinb[:].rearrange("p (h t d) -> p h t d", h=8, t=2)
        prog.on("vector", lambda e, _a=ta3, _b=tb3, _s=sb3: e.tensor_mul(
            _b[:, :, 0, :], _a[:, :, 1, :], _s[:, :, 0, :]))
        prog.phase()
        prog.on("vector", lambda e, _a=ta3, _b=tb3, _s=sb3: e.tensor_mul(
            _b[:, :, 1, :], _a[:, :, 0, :], _s[:, :, 1, :]))
        prog.phase()
        prog.on("vector", lambda e, _c=cosb: e.tensor_mul(ta[:], ta[:], _c[:]))
        prog.phase()
        prog.on("vector", lambda e, _d=dest: e.tensor_add(_d[:], ta[:], tb[:]))

    for ti in range(NTI):
        t0 = ti * 128
        prog.phase()
        prog.dma("sync", cq[:], csq[ti])
        prog.dma("sync", sq[:], ssq[ti])
        prog.dma("gpsimd", ck[:], csk[ti])
        prog.dma("gpsimd", sk[:], ssk[ti])
        bks = [ctx.bank() for _ in range(3)]
        def mmqkv(e, _ti=ti, _t0=t0, _bks=bks):
            ins = None
            for c in range(3):
                for k in range(8):
                    if k < 4:
                        lhs = xt[k][:, _t0:_t0 + 128]
                    elif _ti == 0:
                        lhs = h0[k - 4][:]
                    else:
                        lhs = xt[k][:, _t0 - 1:_t0 + 127]
                    ins = e.matmul(_bks[c][0:128, 0:512], lhs,
                                   wt[k][:, c * 512:(c + 1) * 512],
                                   start=(k == 0), stop=(k == 7))
            return ins
        prog.on("tensor", mmqkv)
        prog.phase()
        for c in range(3):
            prog.on("vector", lambda e, _c=c, _bks=bks: e.tensor_copy(
                qkvt[:, _c * 512:(_c + 1) * 512], _bks[_c][0:128, 0:512]))
        l2norm_rotary(0, qsct, tq, cq, sq)
        l2norm_rotary(512, ksct, tk, ck, sk)
        prog.phase()
        v3 = qkvt[:, 1024:1536].rearrange("p (h d) -> p h d", h=8)
        vt3 = vt1[:].rearrange("p (h d) -> p h d", h=8)
        prog.on("vector", lambda e, _v=v3, _o=vt3: e.tensor_copy(_o[:, :, 0:64], _v))
        prog.on("gpsimd", lambda e, _o=vt3: e.memset(_o[:, :, 64:65], 1.0))
        for which, src, dst in (("q", tq, qf), ("k", tk, kf)):
            for fi in range(4):
                bank = ctx.bank()
                prog.phase()
                prog.on("tensor", lambda e, _s=src, _fi=fi, _b=bank: e.transpose(
                    _b[0:128, 0:128], _s[:, _fi * 128:(_fi + 1) * 128], ident[:]))
                prog.phase()
                prog.on("vector", lambda e, _b=bank: e.tensor_copy(scr[:], _b[0:128, 0:128]))
                prog.phase()
                prog.dma("gpsimd", dst[fi][:, ti * 128:(ti + 1) * 128], scr[:])
        prog.phase()
        prog.dma("gpsimd", vt[ti], vt1[:])
        prog.phase()
    return prog.build()


def build_l1att():
    nc = bass.Bass("TRN2", target_bir_lowering=False)
    qf = nc.dram_tensor("qf", [4, 128, TOV], F32, kind="ExternalInput").ap()
    kf = nc.dram_tensor("kf", [4, 128, TOV], F32, kind="ExternalInput").ap()
    vt = nc.dram_tensor("vt", [NTI, 128, 520], F32, kind="ExternalInput").ap()
    mF = nc.dram_tensor("mF", [4, 128, 512], F32, kind="ExternalInput").ap()
    mR = nc.dram_tensor("mR", [8, 128, 512], F32, kind="ExternalInput").ap()
    out = nc.dram_tensor("out", [4, 128, T], F32, kind="ExternalOutput").ap()
    prog = Prog(nc)
    ctx = Ctx(nc)
    qt = [sb(nc, f"q{i}", 128, TOV) for i in range(4)]
    kt_ = [sb(nc, f"k{i}", 128, TOV) for i in range(4)]
    vtt = [sb(nc, f"v{i}", 128, 520) for i in range(NTI)]
    mFt = [sb(nc, f"mF{i}", 128, 512) for i in range(4)]
    mRt = [sb(nc, f"mR{i}", 128, 512) for i in range(8)]
    att = [sb(nc, f"at{i}", 128, 3 * 512) for i in range(3)]
    setA = [ctx.reserve() for _ in range(3)]
    setB = [ctx.reserve() for _ in range(3)]
    obank = ctx.reserve()
    ofm = [sb(nc, f"of{i}", 128, T) for i in range(4)]
    den = sb(nc, "den", 1, 512)
    zb = sb(nc, "zb", 64, 512)
    ones64 = sb(nc, "o64", 1, 64)
    prog.on("gpsimd", lambda e: e.memset(ones64[:], 1.0))
    for i in range(4):
        prog.dma("sync", qt[i][:], qf[i])
        prog.dma("sync", kt_[i][:], kf[i])
        prog.dma("scalar", mFt[i][:], mF[i])
    for i in range(NTI):
        prog.dma("scalar" if i % 2 else "gpsimd", vtt[i][:], vt[i])
    for i in range(8):
        prog.dma("gpsimd", mRt[i][:], mR[i])
    groups = [[0, 1, 2], [3, 4, 5], [6, 7]]
    ng = len(groups)
    for h in range(8):
        fi, pb = h // 2, 64 * (h % 2)
        for w in range(4):
            q0 = 640 + 512 * w
            for idx in range(ng + 3):
                prog.phase()
                if idx < ng:
                    bset = setA if idx % 2 == 0 else setB
                    def mm_s(e, _g=groups[idx], _bs=bset, _fi=fi, _pb=pb, _q0=q0, _w=w):
                        ins = None
                        for j, kti in enumerate(_g):
                            kcol = 128 + (_w * 4 + kti) * 128
                            ins = e.matmul(_bs[j][0:128, 0:512],
                                           kt_[_fi][_pb:_pb + 64, kcol:kcol + 128],
                                           qt[_fi][_pb:_pb + 64, _q0:_q0 + 512],
                                           start=True, stop=True)
                        return ins
                    prog.on("tensor", mm_s)
                if 0 <= idx - 1 < ng:
                    g = idx - 1
                    bset = setA if g % 2 == 0 else setB
                    for j, kti in enumerate(groups[g]):
                        mask = mFt[kti] if (w == 0 and kti < 4) else mRt[kti]
                        prog.on("vector", lambda e, _g=g, _j=j, _bs=bset, _m=mask: e.scalar_tensor_tensor(
                            att[_g % 3][:, _j * 512:(_j + 1) * 512],
                            _bs[_j][0:128, 0:512], 8.0, _m[:], ALU.mult, ALU.add))
                if 0 <= idx - 2 < ng:
                    g = idx - 2
                    for j in range(len(groups[g])):
                        prog.on("scalar", lambda e, _g=g, _j=j: e.activation(
                            att[_g % 3][:, _j * 512:(_j + 1) * 512],
                            att[_g % 3][:, _j * 512:(_j + 1) * 512], AF.Exp))
                if 0 <= idx - 3 < ng:
                    g = idx - 3
                    def mm_av(e, _g=g, _w=w, _h=h):
                        ins = None
                        for j, kti in enumerate(groups[_g]):
                            vtile = vtt[1 + _w * 4 + kti]
                            v3 = vtile[:].rearrange("p (hh d) -> p hh d", hh=8)
                            ins = e.matmul(obank[0:65, 0:512],
                                           v3[:, _h, :],
                                           att[_g % 3][:, j * 512:(j + 1) * 512],
                                           start=(_g == 0 and j == 0),
                                           stop=(_g == ng - 1 and j == len(groups[_g]) - 1))
                        return ins
                    prog.on("tensor", mm_av)
            prog.phase()
            prog.on("vector", lambda e: e.tensor_copy(den[:], obank[64:65, 0:512]))
            prog.phase()
            prog.on("vector", lambda e: e.reciprocal(den[:], den[:]))
            prog.phase()
            prog.on("tensor", lambda e: e.matmul(
                setA[0][0:64, 0:512], ones64[0:1, :], den[0:1, :], start=True, stop=True))
            prog.phase()
            prog.on("vector", lambda e: e.tensor_copy(zb[:], setA[0][0:64, 0:512]))
            prog.phase()
            prog.on("vector", lambda e, _fi=fi, _pb=pb, _w=w: e.tensor_mul(
                ofm[_fi][_pb:_pb + 64, _w * 512:(_w + 1) * 512],
                obank[0:64, 0:512], zb[:]))
            prog.phase()
    prog.phase()
    for i in range(4):
        prog.dma("gpsimd", out[i], ofm[i][:])
    return prog.build()


def build_shiftln():
    nc = bass.Bass("TRN2", target_bir_lowering=False)
    TI = T + 128
    x = nc.dram_tensor("x", [8, 128, TI], F32, kind="ExternalInput").ap()
    gd = nc.dram_tensor("g", [8, 128, 1], F32, kind="ExternalInput").ap()
    bd = nc.dram_tensor("b", [8, 128, 1], F32, kind="ExternalInput").ap()
    xn = nc.dram_tensor("xn", [8, 128, T], F32, kind="ExternalOutput").ap()
    prog = Prog(nc)
    ctx = Ctx(nc)
    xt = [sb(nc, f"x{i}", 128, TI) for i in range(8)]
    scr = [sb(nc, f"scr{i}", 128, T) for i in range(2)]
    gt = sb(nc, "gt", 128, 8)
    btt = sb(nc, "btt", 128, 8)
    ones1 = sb(nc, "ones1", 128, 1)
    onesr = sb(nc, "onesr", 1, 128)
    mean = sb(nc, "mean", 1, T)
    exsq = sb(nc, "exsq", 1, T)
    m2 = sb(nc, "m2", 1, T)
    var = sb(nc, "var", 1, T)
    mb = sb(nc, "mb", 128, T)
    rb = sb(nc, "rb", 128, T)
    ot = [sb(nc, f"o{i}", 128, T) for i in range(2)]

    def hsl(k):
        return xt[k][:, 128:128 + T] if k < 4 else xt[k][:, 127:127 + T]

    for i in range(8):
        prog.dma("sync", xt[i][:], x[i])
        prog.dma("scalar", gt[:, i:i + 1], gd[i])
        prog.dma("scalar", btt[:, i:i + 1], bd[i])
    prog.on("gpsimd", lambda e: e.memset(ones1[:], 1.0))
    prog.on("gpsimd", lambda e: e.memset(onesr[:], 1.0))
    prog.phase()
    sbank = [ctx.reserve() for _ in range(4)]
    qbank = [ctx.reserve() for _ in range(4)]
    def mm_sum(eng):
        ins = None
        for c in range(4):
            for k in range(8):
                ins = eng.matmul(sbank[c][0:1, :], ones1[:, 0:1],
                                 hsl(k)[:, c * 512:(c + 1) * 512],
                                 start=(k == 0), stop=(k == 7))
        return ins
    prog.on("tensor", mm_sum)
    prog.on("scalar", lambda e: e.activation(scr[0][:], hsl(0), AF.Square))
    for k in range(1, 9):
        prog.phase()
        if k < 8:
            prog.on("scalar", lambda e, _k=k: e.activation(scr[_k % 2][:], hsl(_k), AF.Square))
        def mm_sq(eng, _k=k - 1):
            ins = None
            for c in range(4):
                ins = eng.matmul(qbank[c][0:1, :], ones1[:, 0:1],
                                 scr[_k % 2][:, c * 512:(c + 1) * 512],
                                 start=(_k == 0), stop=(_k == 7))
            return ins
        prog.on("tensor", mm_sq)
    prog.phase()
    for c in range(4):
        prog.on("scalar", lambda e, _c=c: e.activation(
            mean[:, _c * 512:(_c + 1) * 512], sbank[_c][0:1, :], AF.Copy, scale=1.0 / D))
    prog.phase()
    for c in range(4):
        prog.on("scalar", lambda e, _c=c: e.activation(
            exsq[:, _c * 512:(_c + 1) * 512], qbank[_c][0:1, :], AF.Copy, scale=1.0 / D))
    prog.on("vector", lambda e: e.tensor_mul(m2[:], mean[:], mean[:]))
    prog.phase()
    prog.on("vector", lambda e: e.tensor_sub(var[:], exsq[:], m2[:]))
    prog.phase()
    prog.on("vector", lambda e: e.tensor_scalar_add(var[:], var[:], 1e-5))
    prog.phase()
    prog.on("scalar", lambda e: e.activation(var[:], var[:], AF.Sqrt))
    prog.phase()
    prog.on("vector", lambda e: e.reciprocal(var[:], var[:]))
    prog.phase()
    def mm_b(eng):
        ins = None
        for c in range(4):
            ins = eng.matmul(sbank[c][:, :], onesr[0:1, :],
                             mean[:, c * 512:(c + 1) * 512], start=True, stop=True)
            ins = eng.matmul(qbank[c][:, :], onesr[0:1, :],
                             var[:, c * 512:(c + 1) * 512], start=True, stop=True)
        return ins
    prog.on("tensor", mm_b)
    prog.phase()
    for c in range(4):
        prog.on("scalar", lambda e, _c=c: e.activation(
            mb[:, _c * 512:(_c + 1) * 512], sbank[_c][:, :], AF.Copy))
        prog.on("vector", lambda e, _c=c: e.tensor_copy(
            rb[:, _c * 512:(_c + 1) * 512], qbank[_c][:, :]))
    for i in range(8):
        prog.phase()
        prog.on("vector", lambda e, _i=i: e.tensor_sub(ot[_i % 2][:], hsl(_i), mb[:]))
        prog.phase()
        prog.on("vector", lambda e, _i=i: e.tensor_mul(ot[_i % 2][:], ot[_i % 2][:], rb[:]))
        prog.phase()
        prog.on("vector", lambda e, _i=i: e.tensor_scalar(
            ot[_i % 2][:], ot[_i % 2][:], gt[:, _i:_i + 1], btt[:, _i:_i + 1],
            ALU.mult, ALU.add))
        prog.phase()
        prog.dma("gpsimd", xn[i], ot[i % 2][:])
    return prog.build()


def build_glu():
    nc = bass.Bass("TRN2", target_bir_lowering=False)
    HID = 2816
    NS = 11
    x = nc.dram_tensor("x", [8, 128, T], F32, kind="ExternalInput").ap()
    rd = nc.dram_tensor("r", [8, 128, T], F32, kind="ExternalInput").ap()
    w1a = nc.dram_tensor("w1a", [8, 128, HID], F32, kind="ExternalInput").ap()
    w1g = nc.dram_tensor("w1g", [8, 128, HID], F32, kind="ExternalInput").ap()
    w2 = nc.dram_tensor("w2", [22, 128, 1024], F32, kind="ExternalInput").ap()
    b1a = nc.dram_tensor("b1a", [128, 22], F32, kind="ExternalInput").ap()
    b1g = nc.dram_tensor("b1g", [128, 22], F32, kind="ExternalInput").ap()
    b2 = nc.dram_tensor("b2", [128, 8], F32, kind="ExternalInput").ap()
    out = nc.dram_tensor("out", [8, 128, T], F32, kind="ExternalOutput").ap()
    prog = Prog(nc)
    ctx = Ctx(nc)
    xt = [sb(nc, f"x{i}", 128, T) for i in range(8)]
    acc = [sb(nc, f"ac{i}", 128, T) for i in range(8)]
    wa = sb(nc, "wa", 128, 8 * 256)
    wg = sb(nc, "wg", 128, 8 * 256)
    w2s = sb(nc, "w2s", 128, 2 * 1024)
    ua = [sb(nc, f"ua{i}", 128, 512) for i in range(2)]
    ug = [sb(nc, f"ug{i}", 128, 512) for i in range(2)]
    bat = sb(nc, "bat", 128, 22)
    bgt = sb(nc, "bgt", 128, 22)
    b2t = sb(nc, "b2t", 128, 8)
    for i in range(8):
        prog.dma("sync", xt[i][:], x[i])
        prog.dma("gpsimd", acc[i][:], rd[i])
    prog.dma("scalar", bat[:], b1a)
    prog.dma("scalar", bgt[:], b1g)
    prog.dma("scalar", b2t[:], b2)
    prog.phase()
    for i in range(8):
        prog.on("vector", lambda e, _i=i: e.tensor_scalar_add(acc[_i][:], acc[_i][:], b2t[:, _i:_i + 1]))
    for s in range(NS):
        prog.phase()
        for k in range(8):
            prog.dma("sync", wa[:, k * 256:(k + 1) * 256], w1a[k][:, s * 256:(s + 1) * 256])
            prog.dma("gpsimd", wg[:, k * 256:(k + 1) * 256], w1g[k][:, s * 256:(s + 1) * 256])
        for k in range(2):
            prog.dma("scalar", w2s[:, k * 1024:(k + 1) * 1024], w2[s * 2 + k])
        for tc in range(4):
            n0 = tc * 512
            prog.phase()
            ab = [ctx.bank(), ctx.bank()]
            def mma(e, _n0=n0, _bk=ab):
                ins = None
                for mi in range(2):
                    for k in range(8):
                        ins = e.matmul(_bk[mi][0:128, 0:512],
                                       wa[:, k * 256 + mi * 128:k * 256 + mi * 128 + 128],
                                       xt[k][:, _n0:_n0 + 512],
                                       start=(k == 0), stop=(k == 7))
                return ins
            prog.on("tensor", mma)
            prog.phase()
            gb = [ctx.bank(), ctx.bank()]
            def mmg(e, _n0=n0, _bk=gb):
                ins = None
                for mi in range(2):
                    for k in range(8):
                        ins = e.matmul(_bk[mi][0:128, 0:512],
                                       wg[:, k * 256 + mi * 128:k * 256 + mi * 128 + 128],
                                       xt[k][:, _n0:_n0 + 512],
                                       start=(k == 0), stop=(k == 7))
                return ins
            prog.on("tensor", mmg)
            for mi in range(2):
                prog.on("vector", lambda e, _mi=mi, _bk=ab, _s=s: e.tensor_scalar_add(
                    ua[_mi][:], _bk[_mi][0:128, 0:512], bat[:, _s * 2 + _mi:_s * 2 + _mi + 1]))
            prog.phase()
            for mi in range(2):
                prog.on("scalar", lambda e, _mi=mi, _bk=gb, _s=s: e.activation(
                    ug[_mi][:], _bk[_mi][0:128, 0:512], AF.Gelu,
                    bias=bgt[:, _s * 2 + _mi:_s * 2 + _mi + 1]))
            prog.phase()
            for mi in range(2):
                prog.on("vector", lambda e, _mi=mi: e.tensor_mul(ua[_mi][:], ua[_mi][:], ug[_mi][:]))
            prog.phase()
            ob = [ctx.bank() for _ in range(4)]
            def mmw2a(e, _bk=ob):
                ins = None
                for m in range(4):
                    for k in range(2):
                        ins = e.matmul(_bk[m][0:128, 0:512],
                                       w2s[:, k * 1024 + m * 128:k * 1024 + m * 128 + 128],
                                       ua[k][:], start=(k == 0), stop=(k == 1))
                return ins
            prog.on("tensor", mmw2a)
            prog.phase()
            ob2 = [ctx.bank() for _ in range(4)]
            def mmw2b(e, _bk=ob2):
                ins = None
                for m in range(4, 8):
                    for k in range(2):
                        ins = e.matmul(_bk[m - 4][0:128, 0:512],
                                       w2s[:, k * 1024 + m * 128:k * 1024 + m * 128 + 128],
                                       ua[k][:], start=(k == 0), stop=(k == 1))
                return ins
            prog.on("tensor", mmw2b)
            for m in range(4):
                prog.on("vector", lambda e, _m=m, _n0=n0, _bk=ob: e.tensor_add(
                    acc[_m][:, _n0:_n0 + 512], acc[_m][:, _n0:_n0 + 512], _bk[_m][0:128, 0:512]))
            prog.phase()
            for m in range(4, 8):
                prog.on("vector", lambda e, _m=m, _n0=n0, _bk=ob2: e.tensor_add(
                    acc[_m][:, _n0:_n0 + 512], acc[_m][:, _n0:_n0 + 512], _bk[_m - 4][0:128, 0:512]))
    prog.phase()
    for i in range(8):
        prog.dma("gpsimd", out[i], acc[i][:])
    return prog.build()


def _l1ffn_rotary_consts():
    DIM_HEAD, WINDOW = 64, 512
    half = np.arange(0, DIM_HEAD, 2, dtype=np.float64)
    inv_freq = 1.0 / 10000.0 ** (half / DIM_HEAD)
    base = (half + 0.4 * DIM_HEAD) / (1.4 * DIM_HEAD)
    out = []
    for c in range(NCORES):
        hf = c % 2
        tpos = np.arange(TOV, dtype=np.float64) + hf * T - 640
        freqs = tpos[:, None] * inv_freq[None, :]
        freqs = np.concatenate([freqs, freqs], -1)
        power = (tpos - L // 2) / float(WINDOW // 2)
        scale = base[None, :] ** power[:, None]
        scale = np.concatenate([scale, scale], -1)
        cos, sin = np.cos(freqs), np.sin(freqs)
        sgn = np.concatenate([-np.ones(32), np.ones(32)])
        csqv = (cos * scale).astype(np.float32)
        ssqv = (sin * scale * sgn[None, :]).astype(np.float32)
        cskv = (cos / scale).astype(np.float32)
        sskv = (sin / scale * sgn[None, :]).astype(np.float32)
        def tile(v):
            vv = np.tile(v, (1, 8))
            return np.ascontiguousarray(vv.reshape(NTI, 128, 512))
        out.append((tile(csqv), tile(ssqv), tile(cskv), tile(sskv)))
    return out


_ROT_CACHE = None


def l1ffn_layer(x_parts_fm, wqkv, q_scale, k_scale, wo, ln_g, ln_b, w1, b1, w2, b2):
    global _ROT_CACHE
    if _ROT_CACHE is None:
        _ROT_CACHE = _l1ffn_rotary_consts()
    xf = unshard_tokens([from_fm(p) for p in x_parts_fm])
    wqkvT = lhsT_tiles(np.ascontiguousarray(wqkv.astype(np.float32)))
    idn = np.eye(128, dtype=np.float32)
    NEG = np.float32(-np.finfo(np.float32).max)
    kk = np.arange(1024)[:, None]
    qi = np.arange(512)[None, :]
    maskR = np.where((kk < qi) | (kk > qi + 512), NEG, 0.0).astype(np.float32)
    maskR_t = np.ascontiguousarray(maskR.reshape(8, 128, 512))
    maskF_even = np.full((4, 128, 512), NEG, np.float32)
    maskF_odd = np.ascontiguousarray(maskR_t[:4])
    in_maps = []
    for c in range(NCORES):
        b, hf = c // 2, c % 2
        blk = np.zeros((TOV, D), np.float32)
        blk[640:] = xf[b, hf * T:(hf + 1) * T]
        if hf == 1:
            blk[:640] = xf[b, T - 640:T]
        csq, ssq, csk, ssk = _ROT_CACHE[c]
        in_maps.append({
            "x": to_fm(blk), "wqkv": wqkvT,
            "qsc": np.ascontiguousarray(np.broadcast_to(np.tile(q_scale.astype(np.float32), 8), (128, 512))),
            "ksc": np.ascontiguousarray(np.broadcast_to(np.tile(k_scale.astype(np.float32), 8), (128, 512))),
            "csq": csq, "ssq": ssq, "csk": csk, "ssk": ssk, "idn": idn})
    nc = get_prog("l1qkv", build_l1qkv)
    res = run8(nc, in_maps)
    in2 = []
    for c in range(NCORES):
        hf = c % 2
        in2.append({"qf": res[c]["qf"], "kf": res[c]["kf"], "vt": res[c]["vt"],
                    "mF": maskF_even if hf == 0 else maskF_odd, "mR": maskR_t})
    nc2 = get_prog("l1att", build_l1att)
    res2 = run8(nc2, in2)
    o_parts = [r["out"] for r in res2]
    woT = np.ascontiguousarray(wo.astype(np.float32))
    xmid = stage_proj(o_parts, woT, resid_parts_fm=x_parts_fm, stream_out=True)
    xmf = unshard_tokens([from_fm(p) for p in xmid])
    in3 = []
    gdt = np.ascontiguousarray(ln_g.astype(np.float32).reshape(8, 128, 1))
    bdt = np.ascontiguousarray(ln_b.astype(np.float32).reshape(8, 128, 1))
    for c in range(NCORES):
        b, hf = c // 2, c % 2
        blk = np.zeros((T + 128, D), np.float32)
        blk[128:] = xmf[b, hf * T:(hf + 1) * T]
        if hf == 1:
            blk[:128] = xmf[b, T - 128:T]
        in3.append({"x": to_fm(blk), "g": gdt, "b": bdt})
    nc3 = get_prog("shiftln", build_shiftln)
    res3 = run8(nc3, in3)
    hn = [r["xn"] for r in res3]
    HIDR = 2730
    w1a = np.zeros((D, 2816), np.float32); w1a[:, :HIDR] = w1[:, :HIDR]
    w1g = np.zeros((D, 2816), np.float32); w1g[:, :HIDR] = w1[:, HIDR:]
    b1a = np.zeros(2816, np.float32); b1a[:HIDR] = b1[:HIDR]
    b1g = np.zeros(2816, np.float32); b1g[:HIDR] = b1[HIDR:]
    w2p = np.zeros((2816, 1024), np.float32); w2p[:HIDR] = w2
    nc4 = get_prog("glu", build_glu)
    in4 = []
    for c in range(NCORES):
        in4.append({"x": hn[c], "r": xmid[c],
                    "w1a": lhsT_tiles(w1a), "w1g": lhsT_tiles(w1g),
                    "w2": lhsT_tiles(w2p),
                    "b1a": np.ascontiguousarray(b1a.reshape(22, 128).T),
                    "b1g": np.ascontiguousarray(b1g.reshape(22, 128).T),
                    "b2": np.ascontiguousarray(b2.astype(np.float32).reshape(8, 128).T)})
    res4 = run8(nc4, in4)
    return [r["out"] for r in res4]


def l1_ffn_apply(inp, xn_fm):
    cur = xn_fm
    for l in range(2):
        cur = l1ffn_layer(cur, inp["L1_ffn_wqkv"][l], inp["L1_ffn_q_scale"][l],
                          inp["L1_ffn_k_scale"][l], inp["L1_ffn_wo"][l],
                          inp["L1_ffn_ln_g"][l], inp["L1_ffn_ln_b"][l],
                          inp["L1_ffn_w1"][l], inp["L1_ffn_b1"][l],
                          inp["L1_ffn_w2"][l], inp["L1_ffn_b2"][l])
    return cur


def build_l2nh(denom, eps, scale, has_nw):
    nc = bass.Bass("TRN2", target_bir_lowering=False)
    x = nc.dram_tensor("x", [8, 128, T], F32, kind="ExternalInput").ap()
    if has_nw:
        nwd = nc.dram_tensor("nw", [8, 128, 1], F32, kind="ExternalInput").ap()
    out = nc.dram_tensor("out", [8, 128, T], F32, kind="ExternalOutput").ap()
    prog = Prog(nc)
    ctx = Ctx(nc)
    xt = [sb(nc, f"x{i}", 128, T) for i in range(8)]
    nwt = sb(nc, "nwt", 128, 8) if has_nw else None
    sq = sb(nc, "sq", 128, T)
    ones1 = sb(nc, "ones1", 128, 1)
    onesr64 = sb(nc, "onesr64", 1, 64)
    ssum = sb(nc, "ssum", 1, T)
    rbb = sb(nc, "rbb", 128, T)
    for i in range(8):
        prog.dma("sync", xt[i][:], x[i])
        if has_nw:
            prog.dma("scalar", nwt[:, i:i + 1], nwd[i])
    prog.on("gpsimd", lambda e: e.memset(ones1[:], 1.0))
    prog.on("gpsimd", lambda e: e.memset(onesr64[:], 1.0))
    for i in range(8):
        for h2 in range(2):
            pb = 64 * h2
            prog.phase()
            prog.on("scalar", lambda e, _i=i, _p=pb: e.activation(
                sq[_p:_p + 64, :], xt[_i][_p:_p + 64, :], AF.Square))
            prog.phase()
            bks = [ctx.bank() for _ in range(4)]
            def mms(e, _p=pb, _bks=bks):
                ins = None
                for c in range(4):
                    ins = e.matmul(_bks[c][0:1, 0:512], ones1[_p:_p + 64, 0:1],
                                   sq[_p:_p + 64, c * 512:(c + 1) * 512],
                                   start=True, stop=True)
                return ins
            prog.on("tensor", mms)
            prog.phase()
            for c in range(4):
                prog.on("vector", lambda e, _c=c, _bks=bks: e.tensor_scalar(
                    ssum[:, _c * 512:(_c + 1) * 512], _bks[_c][0:1, 0:512],
                    1.0 / denom, eps, ALU.mult, ALU.add))
            prog.phase()
            prog.on("scalar", lambda e: e.activation(ssum[:], ssum[:], AF.Sqrt))
            prog.phase()
            prog.on("vector", lambda e: e.reciprocal(ssum[:], ssum[:]))
            prog.phase()
            bks2 = [ctx.bank() for _ in range(4)]
            def mmb(e, _bks=bks2):
                ins = None
                for c in range(4):
                    ins = e.matmul(_bks[c][0:64, 0:512], onesr64[0:1, :],
                                   ssum[0:1, c * 512:(c + 1) * 512],
                                   start=True, stop=True)
                return ins
            prog.on("tensor", mmb)
            prog.phase()
            for c in range(4):
                prog.on("scalar", lambda e, _c=c, _bks=bks2, _p=pb: e.activation(
                    rbb[_p:_p + 64, _c * 512:(_c + 1) * 512], _bks[_c][0:64, 0:512],
                    AF.Copy, scale=scale))
            prog.phase()
            prog.on("vector", lambda e, _i=i, _p=pb: e.tensor_mul(
                xt[_i][_p:_p + 64, :], xt[_i][_p:_p + 64, :], rbb[_p:_p + 64, :]))
            if has_nw:
                prog.phase()
                prog.on("vector", lambda e, _i=i, _p=pb: e.tensor_scalar(
                    xt[_i][_p:_p + 64, :], xt[_i][_p:_p + 64, :],
                    nwt[_p:_p + 64, _i:_i + 1], None, ALU.mult))
    prog.phase()
    for i in range(8):
        prog.dma("gpsimd", out[i], xt[i][:])
    return prog.build()


def stage_l2nh(x_fm, denom, eps, scale, nw=None):
    nc = get_prog("l2nh", build_l2nh, denom, eps, scale, nw is not None)
    in_maps = []
    for c in range(NCORES):
        m = {"x": x_fm[c]}
        if nw is not None:
            m["nw"] = np.ascontiguousarray(np.tile(nw.astype(np.float32), 16).reshape(8, 128, 1))
        in_maps.append(m)
    res = run8(nc, in_maps)
    return [r["out"] for r in res]


def build_bdec():
    nc = bass.Bass("TRN2", target_bir_lowering=False)
    x = nc.dram_tensor("x", [8, 128, T], F32, kind="ExternalInput").ap()
    wb = nc.dram_tensor("wb", [8, 128, 16], F32, kind="ExternalInput").ap()
    wa = nc.dram_tensor("wa", [8, 128, 16], F32, kind="ExternalInput").ap()
    dtb = nc.dram_tensor("dtb", [128, 16], F32, kind="ExternalInput").ap()
    alog = nc.dram_tensor("alog", [1, 16], F32, kind="ExternalInput").ap()
    bout = nc.dram_tensor("bout", [16, 128, 16], F32, kind="ExternalOutput").ap()
    dout = nc.dram_tensor("dout", [16, 128, 16], F32, kind="ExternalOutput").ap()
    prog = Prog(nc)
    ctx = Ctx(nc)
    xt = [sb(nc, f"x{i}", 128, T) for i in range(8)]
    wbt = sb(nc, "wbt", 128, 8 * 16)
    wat = sb(nc, "wat", 128, 8 * 16)
    dtt = sb(nc, "dtt", 128, 16)
    alt = sb(nc, "alt", 1, 16)
    onesr = sb(nc, "onesr", 1, 128)
    nA = sb(nc, "nA", 128, 16)
    bt_ = sb(nc, "bt_", 128, 16)
    dt_ = sb(nc, "dt_", 128, 16)
    t1 = sb(nc, "t1", 128, 16)
    t2 = sb(nc, "t2", 128, 16)
    for i in range(8):
        prog.dma("sync", xt[i][:], x[i])
        prog.dma("scalar", wbt[:, i * 16:(i + 1) * 16], wb[i])
        prog.dma("scalar", wat[:, i * 16:(i + 1) * 16], wa[i])
    prog.dma("scalar", dtt[:], dtb)
    prog.dma("scalar", alt[:], alog)
    prog.on("gpsimd", lambda e: e.memset(onesr[:], 1.0))
    prog.phase()
    prog.on("scalar", lambda e: e.activation(alt[:], alt[:], AF.Exp))
    prog.phase()
    abank = ctx.reserve()
    prog.on("tensor", lambda e: e.matmul(abank[0:128, 0:16], onesr[0:1, :],
                                         alt[0:1, :], start=True, stop=True))
    prog.phase()
    prog.on("scalar", lambda e: e.activation(nA[:], abank[0:128, 0:16], AF.Copy, scale=-1.0))
    for ti in range(16):
        prog.phase()
        b1, b2 = ctx.bank(), ctx.bank()
        def mmb(e, _ti=ti, _b1=b1, _b2=b2):
            ins = None
            for k in range(8):
                ins = e.matmul(_b1[0:128, 0:16], xt[k][:, _ti * 128:(_ti + 1) * 128],
                               wbt[:, k * 16:(k + 1) * 16], start=(k == 0), stop=(k == 7))
            for k in range(8):
                ins = e.matmul(_b2[0:128, 0:16], xt[k][:, _ti * 128:(_ti + 1) * 128],
                               wat[:, k * 16:(k + 1) * 16], start=(k == 0), stop=(k == 7))
            return ins
        prog.on("tensor", mmb)
        prog.phase()
        prog.on("scalar", lambda e, _b=b1: e.activation(bt_[:], _b[0:128, 0:16], AF.Sigmoid))
        prog.on("vector", lambda e, _b=b2: e.tensor_add(dt_[:], _b[0:128, 0:16], dtt[:]))
        prog.phase()
        prog.on("scalar", lambda e: e.activation(t1[:], dt_[:], AF.Abs))
        prog.phase()
        prog.on("scalar", lambda e: e.activation(t1[:], t1[:], AF.Exp, scale=-1.0))
        prog.phase()
        prog.on("vector", lambda e: e.tensor_scalar_add(t1[:], t1[:], 1.0))
        prog.phase()
        prog.on("scalar", lambda e: e.activation(t1[:], t1[:], AF.Ln))
        prog.phase()
        prog.on("vector", lambda e: e.tensor_scalar_max(t2[:], dt_[:], 0.0))
        prog.phase()
        prog.on("vector", lambda e: e.tensor_add(t1[:], t1[:], t2[:]))
        prog.phase()
        prog.on("vector", lambda e: e.tensor_mul(dt_[:], t1[:], nA[:]))
        prog.phase()
        prog.dma("gpsimd", bout[ti], bt_[:])
        prog.dma("gpsimd", dout[ti], dt_[:])
        prog.phase()
    return prog.build()


NPAIR = 4


def build_dnet(dbg=False):
    nc = bass.Bass("TRN2", target_bir_lowering=False)
    NCH = 64
    qd = nc.dram_tensor("q", [NPAIR, 64, L], F32, kind="ExternalInput").ap()
    ktd = nc.dram_tensor("kt", [NPAIR, 128, 32 * 64], F32, kind="ExternalInput").ap()
    vtd = nc.dram_tensor("vt", [NPAIR, 128, 32 * 64], F32, kind="ExternalInput").ap()
    bd = nc.dram_tensor("bt", [64, NCH * NPAIR], F32, kind="ExternalInput").ap()
    dd = nc.dram_tensor("dt", [64, NCH * NPAIR], F32, kind="ExternalInput").ap()
    trilI = nc.dram_tensor("trilI", [64, 64], F32, kind="ExternalInput").ap()
    trilS = nc.dram_tensor("trilS", [64, 64], F32, kind="ExternalInput").ap()
    tril1 = nc.dram_tensor("tril1", [64, 64], F32, kind="ExternalInput").ap()
    id64 = nc.dram_tensor("id64", [64, 64], F32, kind="ExternalInput").ap()
    od = nc.dram_tensor("o", [NPAIR, L, 64], F32, kind="ExternalOutput").ap()
    if dbg:
        dmp = {n: nc.dram_tensor(f"dbg_{n}", [64, 128], F32, kind="ExternalOutput").ap()
               for n in ("Dm", "Ex", "attn", "Pk", "PkT", "Bm", "vn", "gcol", "gcb")}
    prog = Prog(nc)
    ctx = Ctx(nc)
    TV = ("tensor", "vector", "scalar")
    TVS = ("tensor", "vector", "scalar")
    qf = [sb(nc, f"qf{p}", 64, L) for p in range(NPAIR)]
    kfc = [sb(nc, f"kfc{p}", 64, 64) for p in range(NPAIR)]
    ktm = [sb(nc, f"ktm{p}", 128, 32 * 64) for p in range(NPAIR)]
    vtm = [sb(nc, f"vtm{p}", 128, 32 * 64) for p in range(NPAIR)]
    bsb = sb(nc, "bsb", 64, NCH * NPAIR)
    dsb = sb(nc, "dsb", 64, NCH * NPAIR)
    tI = sb(nc, "tI", 64, 64)
    tS = sb(nc, "tS", 64, 64)
    t1s = sb(nc, "t1s", 64, 64)
    i64 = sb(nc, "i64", 64, 64)
    ident = sb(nc, "ident128", 128, 128)
    S = [sb(nc, f"S{p}", 64, 64) for p in range(NPAIR)]
    gcol = sb(nc, "gcol", 64, NPAIR)
    ones64sq = sb(nc, "ones64sq", 64, 64)
    gb = [sb(nc, f"gb{p}", 64, 64) for p in range(NPAIR)]
    Dm = [sb(nc, f"Dm{p}", 64, 64) for p in range(NPAIR)]
    Ex = [sb(nc, f"Ex{p}", 64, 64) for p in range(NPAIR)]
    attn = [sb(nc, f"attn{p}", 64, 64) for p in range(NPAIR)]
    attnT = [sb(nc, f"attnT{p}", 64, 64) for p in range(NPAIR)]
    Pk = [sb(nc, f"Pk{p}", 64, 64) for p in range(NPAIR)]
    PkT = [sb(nc, f"PkT{p}", 64, 64) for p in range(NPAIR)]
    Bm = [sb(nc, f"Bm{p}", 64, 128) for p in range(NPAIR)]
    kcdf = [sb(nc, f"kcdf{p}", 64, 64) for p in range(NPAIR)]
    vn = [sb(nc, f"vn{p}", 64, 64) for p in range(NPAIR)]
    ktl = [sb(nc, f"ktl{p}", 64, 64) for p in range(NPAIR)]
    osb = [[sb(nc, f"o{p}_{i}", 64, 64) for i in range(2)] for p in range(NPAIR)]
    egc = [sb(nc, f"egc{p}", 64, 1) for p in range(NPAIR)]
    eglc = [sb(nc, f"eglc{p}", 64, 1) for p in range(NPAIR)]
    eglb = [sb(nc, f"eglb{p}", 64, 1) for p in range(NPAIR)]
    dcol = [sb(nc, f"dcol{p}", 64, 1) for p in range(NPAIR)]
    glb = [sb(nc, f"glb{p}", 64, 1) for p in range(NPAIR)]
    bec = [sb(nc, f"bec{p}", 64, 1) for p in range(NPAIR)]
    onesc = sb(nc, "onesc", 1, 64)

    for p in range(NPAIR):
        prog.dma("sync", qf[p][:], qd[p])
        prog.dma("scalar", ktm[p][:], ktd[p])
        prog.dma("scalar", vtm[p][:], vtd[p])
        prog.on("gpsimd", lambda e, _p=p: e.memset(S[_p][:], 0.0))
    prog.dma("gpsimd", bsb[:], bd)
    prog.dma("gpsimd", dsb[:], dd)
    prog.dma("gpsimd", tI[:], trilI)
    prog.dma("gpsimd", tS[:], trilS)
    prog.dma("gpsimd", t1s[:], tril1)
    prog.dma("gpsimd", i64[:], id64)
    prog.on("vector", lambda e: e.memset(ident[:], 0.0))
    prog.on("gpsimd", lambda e: e.memset(onesc[:], 1.0))
    prog.on("gpsimd", lambda e: e.memset(ones64sq[:], 1.0))
    prog.phase()
    prog.on("vector", lambda e: e.tensor_copy(ident[0:64, 0:64], i64[:]))
    prog.on("vector", lambda e: e.tensor_copy(ident[64:128, 64:128], i64[:]))

    def kch(p, c):
        v = ktm[p][:].rearrange("a (t d) -> a t d", d=64)
        return v[64 * (c % 2):64 * (c % 2) + 64, c // 2, :]

    def vch(p, c):
        v = vtm[p][:].rearrange("a (t d) -> a t d", d=64)
        return v[64 * (c % 2):64 * (c % 2) + 64, c // 2, :]

    for c in range(64):
        bsl = lambda p, _c=c: bsb[:, _c * NPAIR + p:_c * NPAIR + p + 1]
        prog.phase(TV)
        gc_ps = ctx.bank()
        def mmg(e, _c=c, _g=gc_ps):
            return e.matmul(_g[0:64, 0:NPAIR], t1s[:, :],
                            dsb[:, _c * NPAIR:(_c + 1) * NPAIR], start=True, stop=True)
        prog.on("tensor", mmg)
        prog.phase(TV)
        prog.on("vector", lambda e, _g=gc_ps: e.tensor_copy(gcol[:], _g[0:64, 0:NPAIR]))
        prog.phase(TV)
        for p in range(NPAIR):
            prog.on("vector", lambda e, _p=p: e.tensor_scalar(
                gb[_p][:], ones64sq[:], gcol[:, _p:_p + 1], None, ALU.mult))
        prog.phase(TV)
        gb_ps = [ctx.bank() for _ in range(NPAIR)]
        kf_ps = [ctx.bank() for _ in range(NPAIR)]
        def mm2b(e, _c=c, _gb=gb_ps, _kf=kf_ps):
            ins = None
            idsl = ident[0:64, 0:64] if _c % 2 == 0 else ident[64:128, 64:128]
            for p in range(NPAIR):
                ins = e.transpose(_gb[p][0:64, 0:64], gb[p][:], ident[0:64, 0:64])
                ins = e.transpose(_kf[p][0:64, 0:64], kch(p, _c), idsl)
            return ins
        prog.on("tensor", mm2b)
        prog.phase(TV)
        for p in range(NPAIR):
            prog.on("vector", lambda e, _p=p, _kf=kf_ps: e.tensor_copy(
                kfc[_p][:], _kf[_p][0:64, 0:64]))
            prog.on("vector", lambda e, _p=p, _gb=gb_ps: e.tensor_copy(
                glb[_p][:], _gb[_p][0:64, 63:64]))
            prog.on("vector", lambda e, _p=p, _gb=gb_ps: e.tensor_scalar(
                Dm[_p][:], _gb[_p][0:64, 0:64], -1.0, gcol[:, _p:_p + 1],
                ALU.mult, ALU.add))
            prog.on("vector", lambda e, _p=p: e.tensor_mul(Dm[_p][:], Dm[_p][:], tI[:]))
            prog.on("vector", lambda e, _p=p, _c=c: e.tensor_sub(
                dcol[_p][:], glb[_p][:], gcol[:, _p:_p + 1]))
        prog.phase(TV)
        qk_ps = [ctx.bank() for _ in range(NPAIR)]
        kk_ps = [ctx.bank() for _ in range(NPAIR)]
        def mm2c(e, _c=c, _qk=qk_ps, _kk=kk_ps):
            ins = None
            for p in range(NPAIR):
                q_ = qf[p][:, _c * 64:(_c + 1) * 64]
                ins = e.matmul(_qk[p][0:64, 0:64], q_, kfc[p][:], start=True, stop=True)
                ins = e.matmul(_kk[p][0:64, 0:64], kfc[p][:], kfc[p][:], start=True, stop=True)
            return ins
        prog.on("tensor", mm2c)
        prog.phase(TVS)
        for p in range(NPAIR):
            prog.on("scalar", lambda e, _p=p: e.activation(Ex[_p][:], Dm[_p][:], AF.Exp))
            prog.on("scalar", lambda e, _p=p, _c=c: e.activation(
                egc[_p][:], gcol[:, _p:_p + 1], AF.Exp))
            prog.on("scalar", lambda e, _p=p: e.activation(eglc[_p][:], dcol[_p][:], AF.Exp))
            prog.on("scalar", lambda e, _p=p: e.activation(eglb[_p][:], glb[_p][:], AF.Exp))
        prog.phase(TV)
        for p in range(NPAIR):
            prog.on("vector", lambda e, _p=p: e.tensor_mul(Ex[_p][:], Ex[_p][:], tI[:]))
            prog.on("vector", lambda e, _p=p, _qk=qk_ps: e.tensor_tensor(attn[_p][:], _qk[_p][0:64, 0:64], Ex[_p][:], ALU.mult))
            prog.on("vector", lambda e, _p=p: e.tensor_mul(Ex[_p][:], Ex[_p][:], tS[:]))
            prog.on("vector", lambda e, _p=p, _kk=kk_ps: e.tensor_tensor(Pk[_p][:], _kk[_p][0:64, 0:64], Ex[_p][:], ALU.mult))
            prog.on("vector", lambda e, _p=p, _b=bsl(p): e.tensor_scalar(
                Pk[_p][:], Pk[_p][:], _b, None, ALU.mult))
            prog.on("vector", lambda e, _p=p, _c=c, _b=bsl(p): e.tensor_scalar(
                Bm[_p][:, 0:64], vch(_p, _c), _b, None, ALU.mult))
            prog.on("vector", lambda e, _p=p, _b=bsl(p): e.tensor_tensor(
                bec[_p][:], _b, egc[_p][:], ALU.mult))
        prog.phase(TV)
        for p in range(NPAIR):
            prog.on("vector", lambda e, _p=p, _c=c: e.tensor_scalar(
                Bm[_p][:, 64:128], kch(_p, _c), bec[_p][:, 0:1], None, ALU.mult))
        prog.phase(TV)
        pt_ps = [ctx.bank() for _ in range(NPAIR)]
        def mmt(e, _pt=pt_ps):
            ins = None
            for p in range(NPAIR):
                ins = e.transpose(_pt[p][0:64, 0:64], Pk[p][:], ident[0:64, 0:64])
            return ins
        prog.on("tensor", mmt)
        prog.phase(TV)
        for p in range(NPAIR):
            prog.on("vector", lambda e, _p=p, _pt=pt_ps: e.tensor_copy(PkT[_p][:], _pt[_p][0:64, 0:64]))
        for j in range(6):
            prog.phase(TV)
            bb_ps = [ctx.bank() for _ in range(NPAIR)]
            p2_ps = [ctx.bank() for _ in range(NPAIR)] if j < 5 else None
            p2t_ps = [ctx.bank() for _ in range(NPAIR)] if j < 5 else None
            def mmi(e, _j=j, _bb=bb_ps, _p2=p2_ps, _p2t=p2t_ps):
                ins = None
                for p in range(NPAIR):
                    ins = e.matmul(_bb[p][0:64, 0:128], PkT[p][:], Bm[p][:],
                                   start=True, stop=True)
                    if _j < 5:
                        ins = e.matmul(_p2[p][0:64, 0:64], PkT[p][:], Pk[p][:],
                                       start=True, stop=True)
                        ins = e.matmul(_p2t[p][0:64, 0:64], Pk[p][:], PkT[p][:],
                                       start=True, stop=True)
                return ins
            prog.on("tensor", mmi)
            prog.phase(TV)
            for p in range(NPAIR):
                prog.on("vector", lambda e, _p=p, _bb=bb_ps: e.tensor_add(
                    Bm[_p][:], Bm[_p][:], _bb[_p][0:64, 0:128]))
                if j < 5:
                    prog.on("vector", lambda e, _p=p, _p2=p2_ps: e.tensor_copy(
                        Pk[_p][:], _p2[_p][0:64, 0:64]))
                    prog.on("vector", lambda e, _p=p, _p2t=p2t_ps: e.tensor_copy(
                        PkT[_p][:], _p2t[_p][0:64, 0:64]))
        prog.phase(TV)
        at_ps = [ctx.bank() for _ in range(NPAIR)]
        kc_ps = [ctx.bank() for _ in range(NPAIR)]
        def mmt2(e, _at=at_ps, _kc=kc_ps):
            ins = None
            for p in range(NPAIR):
                ins = e.transpose(_at[p][0:64, 0:64], attn[p][:], ident[0:64, 0:64])
                ins = e.transpose(_kc[p][0:64, 0:64], Bm[p][:, 64:128], ident[0:64, 0:64])
            return ins
        prog.on("tensor", mmt2)
        prog.phase(TV)
        for p in range(NPAIR):
            prog.on("vector", lambda e, _p=p, _at=at_ps: e.tensor_copy(attnT[_p][:], _at[_p][0:64, 0:64]))
            prog.on("vector", lambda e, _p=p, _kc=kc_ps: e.tensor_copy(kcdf[_p][:], _kc[_p][0:64, 0:64]))
        prog.phase(TV)
        u_ps = [ctx.bank() for _ in range(NPAIR)]
        t1_ps = [ctx.bank() for _ in range(NPAIR)]
        def mmu(e, _c=c, _u=u_ps, _t1=t1_ps):
            ins = None
            for p in range(NPAIR):
                ins = e.matmul(_u[p][0:64, 0:64], kcdf[p][:], S[p][:],
                               start=True, stop=True)
                ins = e.matmul(_t1[p][0:64, 0:64], qf[p][:, _c * 64:(_c + 1) * 64],
                               S[p][:], start=True, stop=True)
            return ins
        prog.on("tensor", mmu)
        prog.phase(TV)
        for p in range(NPAIR):
            prog.on("vector", lambda e, _p=p, _u=u_ps: e.tensor_sub(
                vn[_p][:], Bm[_p][:, 0:64], _u[_p][0:64, 0:64]))
            prog.on("vector", lambda e, _p=p, _c=c, _t1=t1_ps: e.tensor_scalar(
                osb[_p][_c % 2][:], _t1[_p][0:64, 0:64], egc[_p][:, 0:1], None, ALU.mult))
            prog.on("vector", lambda e, _p=p, _c=c: e.tensor_scalar(
                ktl[_p][:], kch(_p, _c), eglc[_p][:, 0:1], None, ALU.mult))
        prog.phase(TV)
        t2_ps = [ctx.bank() for _ in range(NPAIR)]
        st_ps = [ctx.bank() for _ in range(NPAIR)]
        def mm3(e, _t2=t2_ps, _st=st_ps):
            ins = None
            for p in range(NPAIR):
                ins = e.matmul(_t2[p][0:64, 0:64], attnT[p][:], vn[p][:],
                               start=True, stop=True)
                ins = e.matmul(_st[p][0:64, 0:64], ktl[p][:], vn[p][:],
                               start=True, stop=True)
            return ins
        prog.on("tensor", mm3)
        prog.phase(TV)
        for p in range(NPAIR):
            prog.on("vector", lambda e, _p=p, _c=c, _t2=t2_ps: e.tensor_add(
                osb[_p][_c % 2][:], osb[_p][_c % 2][:], _t2[_p][0:64, 0:64]))
            prog.on("vector", lambda e, _p=p, _st=st_ps: e.scalar_tensor_tensor(
                S[_p][:], S[_p][:], eglb[_p][:, 0:1], _st[_p][0:64, 0:64],
                ALU.mult, ALU.add))
        prog.phase(("vector", "gpsimd"))
        for p in range(NPAIR):
            prog.dma("gpsimd", od[p][c * 64:(c + 1) * 64, :], osb[p][c % 2][:])
        if dbg and c == 0:
            prog.phase()
            for n, t in (("Dm", Dm[0]), ("Ex", Ex[0]), ("attn", attn[0]),
                         ("Pk", Pk[0]), ("PkT", PkT[0]), ("vn", vn[0]),
                         ("gcb", gb[0])):
                prog.dma("gpsimd", dmp[n][:, 0:64], t[:])
            prog.dma("gpsimd", dmp["Bm"][:], Bm[0][:])
            prog.dma("gpsimd", dmp["gcol"][:, 0:NPAIR], gcol[:])
            prog.phase()
    return prog.build()


def build_gatemul():
    nc = bass.Bass("TRN2", target_bir_lowering=False)
    a = nc.dram_tensor("a", [8, 128, T], F32, kind="ExternalInput").ap()
    g = nc.dram_tensor("g", [8, 128, T], F32, kind="ExternalInput").ap()
    o = nc.dram_tensor("o", [8, 128, T], F32, kind="ExternalOutput").ap()
    prog = Prog(nc)
    at = [sb(nc, f"a{i}", 128, T) for i in range(2)]
    gt = [sb(nc, f"g{i}", 128, T) for i in range(2)]
    for i in range(8):
        prog.phase()
        prog.dma("sync", at[i % 2][:], a[i])
        prog.dma("scalar", gt[i % 2][:], g[i])
        prog.phase()
        prog.on("scalar", lambda e, _i=i: e.activation(gt[_i % 2][:], gt[_i % 2][:], AF.Silu))
        prog.phase()
        prog.on("vector", lambda e, _i=i: e.tensor_mul(at[_i % 2][:], at[_i % 2][:], gt[_i % 2][:]))
        prog.phase()
        prog.dma("gpsimd", o[i], at[i % 2][:])
    return prog.build()


def l1_attn_apply(inp, x_fm, xn_fm, q=None, k=None, v=None, g=None):
    if q is None:
        q = stage_proj(xn_fm, np.ascontiguousarray(inp["L1_attn_Wq"].astype(np.float32)))
        k = stage_proj(xn_fm, np.ascontiguousarray(inp["L1_attn_Wk"].astype(np.float32)))
        v = stage_proj(xn_fm, np.ascontiguousarray(inp["L1_attn_Wv"].astype(np.float32)))
        g = stage_proj(xn_fm, np.ascontiguousarray(inp["L1_attn_Wg"].astype(np.float32)))
    qn = stage_l2nh(q, 1.0, 1e-6, 0.125)
    kn = stage_l2nh(k, 1.0, 1e-6, 1.0)
    nc = get_prog("bdec", build_bdec)
    wbT = lhsT_tiles(np.ascontiguousarray(inp["L1_attn_Wb"].astype(np.float32)))
    waT = lhsT_tiles(np.ascontiguousarray(inp["L1_attn_Wa"].astype(np.float32)))
    dtb = np.ascontiguousarray(np.broadcast_to(inp["L1_attn_dt_bias"].astype(np.float32), (128, 16)))
    alog = np.ascontiguousarray(inp["L1_attn_A_log"].astype(np.float32).reshape(1, 16))
    resb = run8(nc, [{"x": xn_fm[c], "wb": wbT, "wa": waT, "dtb": dtb, "alog": alog}
                     for c in range(NCORES)])
    beta = unshard_tokens([r["bout"].reshape(T, 16) for r in resb])
    decay = unshard_tokens([r["dout"].reshape(T, 16) for r in resb])
    qf = unshard_tokens([from_fm(p) for p in qn])
    kf = unshard_tokens([from_fm(p) for p in kn])
    vf = unshard_tokens([from_fm(p) for p in v])
    cI = np.tril(np.ones((64, 64), np.float32))
    cS = np.tril(np.ones((64, 64), np.float32), -1)
    c1 = np.triu(np.ones((64, 64), np.float32))
    ci = np.eye(64, dtype=np.float32)
    ncd = get_prog("dnet", build_dnet)
    o_full = np.zeros((B, L, 16, 64), np.float32)
    for li in range(8 // NPAIR):
        in_maps = []
        for c in range(NCORES):
            m = {"q": [], "k": [], "kt": [], "vt": [], "bt": None, "dt": None}
            bcols = []
            dcols = []
            for j in range(NPAIR):
                gg = c * 8 + li * NPAIR + j
                b, h = gg // 16, gg % 16
                sl = slice(h * 64, (h + 1) * 64)
                m["q"].append(qf[b, :, sl].T)
                m["kt"].append(kf[b, :, sl].reshape(32, 128, 64).transpose(1, 0, 2).reshape(128, -1))
                m["vt"].append(vf[b, :, sl].reshape(32, 128, 64).transpose(1, 0, 2).reshape(128, -1))
                bcols.append(beta[b, :, h])
                dcols.append(decay[b, :, h])
            barr = np.stack(bcols, -1).reshape(64, 64, NPAIR).transpose(1, 0, 2).reshape(64, -1)
            darr = np.stack(dcols, -1).reshape(64, 64, NPAIR).transpose(1, 0, 2).reshape(64, -1)
            in_maps.append({"q": np.ascontiguousarray(np.stack(m["q"])),
                            "kt": np.ascontiguousarray(np.stack(m["kt"])),
                            "vt": np.ascontiguousarray(np.stack(m["vt"])),
                            "bt": np.ascontiguousarray(barr),
                            "dt": np.ascontiguousarray(darr),
                            "trilI": cI, "trilS": cS, "tril1": c1, "id64": ci})
        res = run8(ncd, in_maps)
        for c in range(NCORES):
            for j in range(NPAIR):
                gg = c * 8 + li * NPAIR + j
                b, h = gg // 16, gg % 16
                o_full[b, :, h, :] = res[c]["o"][j]
    o_fm = [to_fm(p) for p in shard_tokens(o_full.reshape(B, L, D))]
    onrm = stage_l2nh(o_fm, 64.0, 1e-5, 1.0, nw=inp["L1_attn_norm_w"])
    ncg = get_prog("gatemul", build_gatemul)
    resg = run8(ncg, [{"a": onrm[c], "g": g[c]} for c in range(NCORES)])
    og = [r["o"] for r in resg]
    WoT = np.ascontiguousarray(inp["L1_attn_Wo"].astype(np.float32))
    return stage_proj(og, WoT, resid_parts_fm=x_fm, stream_out=True)


def kernel(**inputs):
    inp = inputs
    x = np.asarray(inp["L0_attn_x"], np.float32)
    x_fm = [to_fm(p) for p in shard_tokens(x)]
    xn, comb0, xmid = l0_block_apply(inp, x_fm)
    x1 = l0_ffn_apply_all(inp, xmid, xn, comb0)
    r1 = stage_front("l1", x1, {
        "wq": lhsT_tiles(np.ascontiguousarray(inp["L1_attn_Wq"].astype(np.float32))),
        "wk": lhsT_tiles(np.ascontiguousarray(inp["L1_attn_Wk"].astype(np.float32))),
        "wv": lhsT_tiles(np.ascontiguousarray(inp["L1_attn_Wv"].astype(np.float32))),
        "wg": lhsT_tiles(np.ascontiguousarray(inp["L1_attn_Wg"].astype(np.float32)))})
    xn1 = [r["xn"] for r in r1]
    xmid1 = l1_attn_apply(inp, x1, xn1, [r["owq"] for r in r1],
                          [r["owk"] for r in r1], [r["owv"] for r in r1],
                          [r["owg"] for r in r1])
    ffn1 = l1_ffn_apply(inp, xn1)
    x2 = stage_add(xmid1, ffn1)
    WT2 = np.ascontiguousarray(inp["L2_attn_W_qkv"].T)
    cols = np.arange(3072).reshape(16, 3, 64)
    r2 = stage_front("l2", x2, {
        "wq": lhsT_tiles(np.ascontiguousarray(WT2[:, cols[:, 0].reshape(-1)])),
        "wk": lhsT_tiles(np.ascontiguousarray(WT2[:, cols[:, 1].reshape(-1)])),
        "wv": lhsT_tiles(np.ascontiguousarray(WT2[:, cols[:, 2].reshape(-1)]))})
    xn2 = [r["xn"] for r in r2]
    t2 = l2_attn_apply(inp, x2, xn2, [r["owq"] for r in r2],
                       [r["owk"] for r in r2], [r["owv"] for r in r2])
    x3 = l2_ffn_apply(inp, t2, xn2)
    w_in3 = inp["L3_attn_w_in"]
    r3 = stage_front("l3", x3, {
        "wxp": lhsT_tiles(np.ascontiguousarray(w_in3[:, :2048].astype(np.float32))),
        "wz": lhsT_tiles(np.ascontiguousarray(w_in3[:, 2048:].astype(np.float32)))})
    xn3 = [r["xn"] for r in r3]
    t3 = l3_attn_apply(inp, x3, xn3, [r["owxp"] for r in r3], [r["owz"] for r in r3])
    x4 = l3_ffn_apply(inp, t3, xn3)
    return unshard_tokens([from_fm(p) for p in x4]).reshape(B, L, D)


def build_moe_all():
    nc = bass.Bass("TRN2", target_bir_lowering=False)
    NE = 9
    x = nc.dram_tensor("x", [8, 128, T], F32, kind="ExternalInput").ap()
    xmid = nc.dram_tensor("xmid", [8, 128, T], F32, kind="ExternalInput").ap()
    combd = nc.dram_tensor("combd", [NE, 128, T], F32, kind="ExternalInput").ap()
    w1g = nc.dram_tensor("w1g", [NE, 8, 128, 2048], F32, kind="ExternalInput").ap()
    w1l = nc.dram_tensor("w1l", [NE, 8, 128, 2048], F32, kind="ExternalInput").ap()
    w3g = nc.dram_tensor("w3g", [NE, 8, 128, 2048], F32, kind="ExternalInput").ap()
    w3l = nc.dram_tensor("w3l", [NE, 8, 128, 2048], F32, kind="ExternalInput").ap()
    w2 = nc.dram_tensor("w2", [NE, 16, 128, 1024], F32, kind="ExternalInput").ap()
    b1g = nc.dram_tensor("b1g", [NE, 128, 16], F32, kind="ExternalInput").ap()
    b1l = nc.dram_tensor("b1l", [NE, 128, 16], F32, kind="ExternalInput").ap()
    b3g = nc.dram_tensor("b3g", [NE, 128, 16], F32, kind="ExternalInput").ap()
    b3l = nc.dram_tensor("b3l", [NE, 128, 16], F32, kind="ExternalInput").ap()
    b2 = nc.dram_tensor("b2", [NE, 128, 8], F32, kind="ExternalInput").ap()
    out = nc.dram_tensor("out", [8, 128, T], F32, kind="ExternalOutput").ap()

    prog = Prog(nc)
    ctx = Ctx(nc)
    xt = [sb(nc, f"x{i}", 128, T) for i in range(8)]
    xacc = [sb(nc, f"xa{i}", 128, T) for i in range(8)]
    comb_sb = sb(nc, "compb", 128, 512)
    wa = sb(nc, "wa", 128, 8 * 256)
    wb = sb(nc, "wb", 128, 8 * 256)
    w2s = [sb(nc, f"w2s{i}", 128, 1024) for i in range(2)]
    G = [sb(nc, f"G{i}", 128, T) for i in range(2)]
    u1 = [sb(nc, f"u1_{i}", 128, T) for i in range(2)]
    u2 = [sb(nc, f"u2_{i}", 128, T) for i in range(2)]
    bb = {n: sb(nc, f"bb_{n}", 128, 16) for n in ("b1g", "b1l", "b3g", "b3l")}
    bb2 = sb(nc, "bb2", 128, 8)
    scr = sb(nc, "scrm", 128, 512)

    for i in range(8):
        prog.dma("sync", xt[i][:], x[i])
        prog.dma("gpsimd", xacc[i][:], xmid[i])

    for e in range(9):
        prog.phase()
        for n, dr in (("b1g", b1g), ("b1l", b1l), ("b3g", b3g), ("b3l", b3l)):
            prog.dma("scalar", bb[n][:], dr[e])
        prog.dma("scalar", bb2[:], b2[e])
        for n0 in range(4):
            prog.phase()
            prog.dma("scalar", comb_sb[:], combd[e][:, n0 * 512:(n0 + 1) * 512])
            prog.phase()
            for m in range(8):
                prog.on("vector", lambda e_, _m=m, _n0=n0: e_.scalar_tensor_tensor(
                    xacc[_m][:, _n0 * 512:(_n0 + 1) * 512], comb_sb[:],
                    bb2[:, _m:_m + 1], xacc[_m][:, _n0 * 512:(_n0 + 1) * 512],
                    ALU.mult, ALU.add))
        for s in range(8):
            for k in range(8):
                prog.dma("sync", wa[:, k * 256:(k + 1) * 256], w1g[e][k][:, s * 256:(s + 1) * 256])
                prog.dma("gpsimd", wb[:, k * 256:(k + 1) * 256], w3g[e][k][:, s * 256:(s + 1) * 256])
            for k in range(2):
                prog.dma("scalar", w2s[k][:], w2[e][s * 2 + k])
            for half, wsb, biasn, dest in ((0, wa, "b1g", u1), (1, wb, "b3g", u2)):
                rbk = [[ctx.bank() for _ in range(4)] for _ in range(2)]
                for rnd in range(3):
                    prog.phase()
                    if rnd < 2:
                        def mmp(e_, _w=wsb, _bks=rbk[rnd], _r=rnd):
                            ins = None
                            for j in range(4):
                                mi, n0 = divmod(_r * 4 + j, 4)
                                for k in range(8):
                                    ins = e_.matmul(
                                        _bks[j][0:128, 0:512],
                                        _w[:, k * 256 + mi * 128:k * 256 + mi * 128 + 128],
                                        xt[k][:, n0 * 512:(n0 + 1) * 512],
                                        start=(k == 0), stop=(k == 7))
                            return ins
                        prog.on("tensor", mmp)
                    if rnd > 0:
                        for j in range(4):
                            mi, n0 = divmod((rnd - 1) * 4 + j, 4)
                            prog.on("vector", lambda e_, _mi=mi, _n0=n0, _j=j,
                                    _bks=rbk[rnd - 1], _s=s, _bn=biasn, _d=dest: e_.tensor_scalar_add(
                                _d[_mi][:, _n0 * 512:(_n0 + 1) * 512],
                                _bks[_j][0:128, 0:512],
                                bb[_bn][:, _s * 2 + _mi:_s * 2 + _mi + 1]))
            prog.phase()
            for mi in range(2):
                prog.on("vector", lambda e_, _mi=mi: e_.tensor_mul(u1[_mi][:], u1[_mi][:], u2[_mi][:]))
                prog.on("vector", lambda e_, _mi=mi: e_.tensor_scalar_min(u1[_mi][:], u1[_mi][:], 7.0))
            prog.phase()
            for mi in range(2):
                prog.on("scalar", lambda e_, _mi=mi: e_.activation(u2[_mi][:], u1[_mi][:], AF.Sigmoid, scale=1.702))
            prog.phase()
            for mi in range(2):
                prog.on("vector", lambda e_, _mi=mi: e_.tensor_mul(G[_mi][:], u1[_mi][:], u2[_mi][:]))
            for k in range(8):
                prog.dma("sync", wa[:, k * 256:(k + 1) * 256], w1l[e][k][:, s * 256:(s + 1) * 256])
                prog.dma("gpsimd", wb[:, k * 256:(k + 1) * 256], w3l[e][k][:, s * 256:(s + 1) * 256])
            for half, wsb, biasn, dest in ((0, wa, "b1l", u1), (1, wb, "b3l", u2)):
                rbk = [[ctx.bank() for _ in range(4)] for _ in range(2)]
                for rnd in range(3):
                    prog.phase()
                    if rnd < 2:
                        def mmp2(e_, _w=wsb, _bks=rbk[rnd], _r=rnd):
                            ins = None
                            for j in range(4):
                                mi, n0 = divmod(_r * 4 + j, 4)
                                for k in range(8):
                                    ins = e_.matmul(
                                        _bks[j][0:128, 0:512],
                                        _w[:, k * 256 + mi * 128:k * 256 + mi * 128 + 128],
                                        xt[k][:, n0 * 512:(n0 + 1) * 512],
                                        start=(k == 0), stop=(k == 7))
                            return ins
                        prog.on("tensor", mmp2)
                    if rnd > 0:
                        for j in range(4):
                            mi, n0 = divmod((rnd - 1) * 4 + j, 4)
                            prog.on("vector", lambda e_, _mi=mi, _n0=n0, _j=j,
                                    _bks=rbk[rnd - 1], _s=s, _bn=biasn, _d=dest: e_.tensor_scalar_add(
                                _d[_mi][:, _n0 * 512:(_n0 + 1) * 512],
                                _bks[_j][0:128, 0:512],
                                bb[_bn][:, _s * 2 + _mi:_s * 2 + _mi + 1]))
            prog.phase()
            for mi in range(2):
                prog.on("vector", lambda e_, _mi=mi: e_.tensor_mul(u1[_mi][:], u1[_mi][:], u2[_mi][:]))
                prog.on("vector", lambda e_, _mi=mi: e_.tensor_scalar(u1[_mi][:], u1[_mi][:], 7.0, -7.0, ALU.min, ALU.max))
                prog.on("vector", lambda e_, _mi=mi: e_.scalar_tensor_tensor(
                    u2[_mi][:], u1[_mi][:], 1.0, G[_mi][:], ALU.add, ALU.mult))
            for n0 in range(4):
                prog.phase()
                obk = [ctx.bank() for _ in range(8)]
                prog.dma("scalar", comb_sb[:], combd[e][:, n0 * 512:(n0 + 1) * 512])
                def mmw(e_, _n0=n0, _bk=obk):
                    ins = None
                    for m in range(8):
                        for k in range(2):
                            ins = e_.matmul(_bk[m][0:128, 0:512],
                                            w2s[k][:, m * 128:m * 128 + 128],
                                            u2[k][:, _n0 * 512:(_n0 + 1) * 512],
                                            start=(k == 0), stop=(k == 1))
                    return ins
                prog.on("tensor", mmw)
                prog.phase()
                for m in range(8):
                    prog.on("vector", lambda e_, _m=m, _n0=n0, _bk=obk: e_.tensor_tensor(
                        scr[:], _bk[_m][0:128, 0:512], comb_sb[:], ALU.mult))
                    prog.on("vector", lambda e_, _m=m, _n0=n0: e_.tensor_add(
                        xacc[_m][:, _n0 * 512:(_n0 + 1) * 512],
                        xacc[_m][:, _n0 * 512:(_n0 + 1) * 512], scr[:]))
    prog.phase()
    for m in range(8):
        prog.dma("gpsimd", out[m], xacc[m][:])
    return prog.build()


def l0_ffn_apply_all(inp, xmid_fm, xn_fm, comb=None):
    if comb is None:
        comb = stage_gate0(xn_fm, inp["L0_ffn_gate_w"], inp["L0_ffn_gate_b"])
    NE = 9

    def prep(w):
        return (lhsT_tiles(np.ascontiguousarray(w[:, 0::2].astype(np.float32))),
                lhsT_tiles(np.ascontiguousarray(w[:, 1::2].astype(np.float32))))
    w1g_l, w1l_l, w3g_l, w3l_l, w2_l = [], [], [], [], []
    b1g_l, b1l_l, b3g_l, b3l_l, b2_l = [], [], [], [], []
    for e in range(8):
        a, b_ = prep(inp["L0_ffn_w1"][e]); w1g_l.append(a); w1l_l.append(b_)
        a, b_ = prep(inp["L0_ffn_w3"][e]); w3g_l.append(a); w3l_l.append(b_)
        w2_l.append(lhsT_tiles(np.ascontiguousarray(inp["L0_ffn_w2"][e].astype(np.float32))))
        b1g_l.append(inp["L0_ffn_b1"][e][0::2]); b1l_l.append(inp["L0_ffn_b1"][e][1::2])
        b3g_l.append(inp["L0_ffn_b3"][e][0::2]); b3l_l.append(inp["L0_ffn_b3"][e][1::2])
        b2_l.append(inp["L0_ffn_b2"][e])
    a, b_ = prep(inp["L0_ffn_sw1"]); w1g_l.append(a); w1l_l.append(b_)
    a, b_ = prep(inp["L0_ffn_sw3"]); w3g_l.append(a); w3l_l.append(b_)
    w2_l.append(lhsT_tiles(np.ascontiguousarray(inp["L0_ffn_sw2"].astype(np.float32))))
    b1g_l.append(inp["L0_ffn_sb1"][0::2]); b1l_l.append(inp["L0_ffn_sb1"][1::2])
    b3g_l.append(inp["L0_ffn_sb3"][0::2]); b3l_l.append(inp["L0_ffn_sb3"][1::2])
    b2_l.append(inp["L0_ffn_sb2"])
    W = {"w1g": np.stack(w1g_l), "w1l": np.stack(w1l_l),
         "w3g": np.stack(w3g_l), "w3l": np.stack(w3l_l), "w2": np.stack(w2_l)}
    Bt = {"b1g": np.stack([np.ascontiguousarray(v.astype(np.float32).reshape(16, 128).T) for v in b1g_l]),
          "b1l": np.stack([np.ascontiguousarray(v.astype(np.float32).reshape(16, 128).T) for v in b1l_l]),
          "b3g": np.stack([np.ascontiguousarray(v.astype(np.float32).reshape(16, 128).T) for v in b3g_l]),
          "b3l": np.stack([np.ascontiguousarray(v.astype(np.float32).reshape(16, 128).T) for v in b3l_l]),
          "b2": np.stack([np.ascontiguousarray(v.astype(np.float32).reshape(8, 128).T) for v in b2_l])}
    nc = get_prog("moe_all", build_moe_all)
    in_maps = []
    for c in range(NCORES):
        cb = np.stack([np.tile(comb[c][:, e], (128, 1)) for e in range(8)]
                      + [np.ones((128, T), np.float32)])
        m = {"x": xn_fm[c], "xmid": xmid_fm[c], "combd": np.ascontiguousarray(cb)}
        m.update(W)
        m.update(Bt)
        in_maps.append(m)
    res = run8(nc, in_maps)
    return [r["out"] for r in res]


def emit_ln(prog, ctx, pre, xt, xnt):
    nc = prog.nc
    scr = [sb(nc, f"{pre}scr{i}", 128, T) for i in range(2)]
    ones1 = sb(nc, f"{pre}o1", 128, 1)
    onesr = sb(nc, f"{pre}or", 1, 128)
    mean = sb(nc, f"{pre}mean", 1, T)
    exsq = scr[1][0:1, 0:T]
    m2 = scr[0][0:1, 0:T]
    var = exsq
    mb = sb(nc, f"{pre}mb", 128, T)
    rb = sb(nc, f"{pre}rb", 128, T)
    prog.on("gpsimd", lambda e: e.memset(ones1[:], 1.0))
    prog.on("gpsimd", lambda e: e.memset(onesr[:], 1.0))
    prog.phase()
    sbank = [ctx.bank() for _ in range(4)]
    qbank = [ctx.bank() for _ in range(4)]
    def mm_sum(eng):
        ins = None
        for c in range(4):
            for k in range(8):
                ins = eng.matmul(sbank[c][0:1, :], ones1[:, 0:1],
                                 xt[k][:, c * 512:(c + 1) * 512],
                                 start=(k == 0), stop=(k == 7))
        return ins
    prog.on("tensor", mm_sum)
    prog.on("scalar", lambda e: e.activation(scr[0][:], xt[0][:], AF.Square))
    for k in range(1, 9):
        prog.phase()
        if k < 8:
            prog.on("scalar", lambda e, _k=k: e.activation(scr[_k % 2][:], xt[_k][:], AF.Square))
        def mm_sq(eng, _k=k - 1):
            ins = None
            for c in range(4):
                ins = eng.matmul(qbank[c][0:1, :], ones1[:, 0:1],
                                 scr[_k % 2][:, c * 512:(c + 1) * 512],
                                 start=(_k == 0), stop=(_k == 7))
            return ins
        prog.on("tensor", mm_sq)
    prog.phase()
    for c in range(4):
        prog.on("scalar", lambda e, _c=c: e.activation(
            mean[:, _c * 512:(_c + 1) * 512], sbank[_c][0:1, :], AF.Copy, scale=1.0 / D))
    prog.phase()
    for c in range(4):
        prog.on("scalar", lambda e, _c=c: e.activation(
            exsq[0:1, _c * 512:(_c + 1) * 512], qbank[_c][0:1, :], AF.Copy, scale=1.0 / D))
    prog.on("vector", lambda e: e.tensor_mul(m2, mean[:], mean[:]))
    prog.phase()
    prog.on("vector", lambda e: e.tensor_sub(var, exsq, m2))
    prog.phase()
    prog.on("vector", lambda e: e.tensor_scalar_add(var, var, 1e-5))
    prog.phase()
    prog.on("scalar", lambda e: e.activation(var, var, AF.Sqrt))
    prog.phase()
    prog.on("vector", lambda e: e.reciprocal(var, var))
    prog.phase()
    def mm_b(eng):
        ins = None
        for c in range(4):
            ins = eng.matmul(sbank[c][:, :], onesr[0:1, :],
                             mean[:, c * 512:(c + 1) * 512], start=True, stop=True)
            ins = eng.matmul(qbank[c][:, :], onesr[0:1, :],
                             scr[1][0:1, c * 512:(c + 1) * 512], start=True, stop=True)
        return ins
    prog.on("tensor", mm_b)
    prog.phase()
    for c in range(4):
        prog.on("scalar", lambda e, _c=c: e.activation(
            mb[:, _c * 512:(_c + 1) * 512], sbank[_c][:, :], AF.Copy))
        prog.on("vector", lambda e, _c=c: e.tensor_copy(
            rb[:, _c * 512:(_c + 1) * 512], qbank[_c][:, :]))
    prog.phase()
    for i in range(8):
        prog.on("vector", lambda e, _i=i: e.tensor_sub(xnt[_i][:], xt[_i][:], mb[:]))
    prog.phase()
    for i in range(8):
        prog.on("vector", lambda e, _i=i: e.tensor_mul(xnt[_i][:], xnt[_i][:], rb[:]))
    prog.phase()


def build_front(kind):
    nc = bass.Bass("TRN2", target_bir_lowering=False)
    specs = {
        "l0": [("wq", 1024, "relu", True), ("wk", 1024, "relu", True),
               ("wv", 1024, "none", True)],
        "l1": [("wq", 1024, "none", False), ("wk", 1024, "none", False),
               ("wv", 1024, "none", False), ("wg", 1024, "none", False)],
        "l2": [("wq", 1024, "none", False), ("wk", 1024, "none", False),
               ("wv", 1024, "none", False)],
        "l3": [("wxp", 2048, "none", False), ("wz", 2048, "none", False)],
    }[kind]
    x = nc.dram_tensor("x", [8, 128, T], F32, kind="ExternalInput").ap()
    wd = {}
    bd = {}
    outd = {}
    for (n, mout, act, hb) in specs:
        wd[n] = nc.dram_tensor(n, [8, 128, mout], F32, kind="ExternalInput").ap()
        if hb:
            bd[n] = nc.dram_tensor("b" + n, [mout // 128, 128, 1], F32,
                                   kind="ExternalInput").ap()
        outd[n] = nc.dram_tensor("o" + n, [mout // 128, 128, T], F32,
                                 kind="ExternalOutput").ap()
    if kind == "l0":
        sinb = nc.dram_tensor("sinb", [128, T], F32, kind="ExternalInput").ap()
        cosb = nc.dram_tensor("cosb", [128, T], F32, kind="ExternalInput").ap()
        gw = nc.dram_tensor("gw", [8, 128, 8], F32, kind="ExternalInput").ap()
        gb = nc.dram_tensor("gb", [128, 8], F32, kind="ExternalInput").ap()
        for n in ("qs", "qc", "ks", "kc"):
            outd[n] = nc.dram_tensor("o" + n, [8, 128, T], F32, kind="ExternalOutput").ap()
        combo = nc.dram_tensor("comb", [16, 128, 8], F32, kind="ExternalOutput").ap()
    xnout = nc.dram_tensor("xn", [8, 128, T], F32, kind="ExternalOutput").ap()
    prog = Prog(nc)
    ctx = Ctx(nc)
    xt = [sb(nc, f"x{i}", 128, T) for i in range(8)]
    xnt = [sb(nc, f"xn{i}", 128, T) for i in range(8)]
    for i in range(8):
        prog.dma("sync", xt[i][:], x[i])
    emit_ln(prog, ctx, "ln_", xt, xnt)
    for i in range(8):
        prog.dma("gpsimd", xnout[i], xnt[i][:])
    wt2 = [[sb(nc, f"wst{j}_{i}", 128, 128) for i in range(8)] for j in range(2)]
    ot = [sb(nc, f"ot{j}", 128, 512) for j in range(2)]
    bt = sb(nc, "btile", 128, 8)
    if kind == "l0":
        snt = sb(nc, "snt", 128, T)
        cst = sb(nc, "cst", 128, T)
        sc_ot = [sb(nc, f"sc_ot{j}", 128, 512) for j in range(2)]
        prog.dma("scalar", snt[:], sinb)
        prog.dma("scalar", cst[:], cosb)
    for (n, mout, act, hb) in specs:
        mtiles = mout // 128
        if hb:
            prog.phase()
            for mi in range(mtiles):
                prog.dma("scalar", bt[:, mi:mi + 1], bd[n][mi])
        for mi in range(mtiles):
            prog.phase()
            for k in range(8):
                prog.dma("sync", wt2[mi % 2][k][:], wd[n][k][:, mi * 128:(mi + 1) * 128])
            for n0 in range(4):
                bank = ctx.bank()
                oslot = ot[n0 % 2] if True else None
                prog.phase()
                def mk(e, _mi=mi, _n0=n0, _b=bank, _w=wt2[mi % 2]):
                    ins = None
                    for k in range(8):
                        ins = e.matmul(_b[0:128, 0:512], _w[k][:],
                                       xnt[k][:, _n0 * 512:(_n0 + 1) * 512],
                                       start=(k == 0), stop=(k == 7))
                    return ins
                prog.on("tensor", mk)
                prog.phase()
                if act == "relu":
                    prog.on("scalar", lambda e, _b=bank, _o=oslot, _mi=mi: e.activation(
                        _o[:], _b[0:128, 0:512], AF.Relu, bias=bt[:, _mi:_mi + 1]))
                elif hb:
                    prog.on("vector", lambda e, _b=bank, _o=oslot, _mi=mi: e.tensor_scalar_add(
                        _o[:], _b[0:128, 0:512], bt[:, _mi:_mi + 1]))
                else:
                    prog.on("vector", lambda e, _b=bank, _o=oslot: e.tensor_copy(
                        _o[:], _b[0:128, 0:512]))
                prog.phase()
                prog.dma("gpsimd", outd[n][mi][:, n0 * 512:(n0 + 1) * 512], oslot[:])
                if kind == "l0" and n in ("wq", "wk"):
                    sn, cn = ("qs", "qc") if n == "wq" else ("ks", "kc")
                    prog.on("vector", lambda e, _o=oslot, _n0=n0: e.tensor_mul(
                        sc_ot[0][:], _o[:], snt[:, _n0 * 512:(_n0 + 1) * 512]))
                    prog.phase()
                    prog.dma("gpsimd", outd[sn][mi][:, n0 * 512:(n0 + 1) * 512], sc_ot[0][:])
                    prog.on("vector", lambda e, _o=oslot, _n0=n0: e.tensor_mul(
                        sc_ot[1][:], _o[:], cst[:, _n0 * 512:(_n0 + 1) * 512]))
                    prog.phase()
                    prog.dma("gpsimd", outd[cn][mi][:, n0 * 512:(n0 + 1) * 512], sc_ot[1][:])
    if kind == "l0":
        gwt = sb(nc, "gwt", 128, 64)
        gbt = sb(nc, "gbt", 128, 8)
        s_sb = sb(nc, "sg_s", 128, 8)
        c_sb = sb(nc, "cg_s", 128, 8)
        t1g = sb(nc, "t1g", 128, 8)
        t2g = sb(nc, "t2g", 128, 8)
        mxg = sb(nc, "mxg", 128, 1)
        smg = sb(nc, "smg", 128, 1)
        prog.phase()
        for i in range(8):
            prog.dma("scalar", gwt[:, i * 8:(i + 1) * 8], gw[i])
        prog.dma("scalar", gbt[:], gb)
        for ti in range(16):
            bank = ctx.bank()
            prog.phase()
            def mkg(e, _ti=ti, _b=bank):
                ins = None
                for k in range(8):
                    ins = e.matmul(_b[0:128, 0:8], xnt[k][:, _ti * 128:(_ti + 1) * 128],
                                   gwt[:, k * 8:(k + 1) * 8], start=(k == 0), stop=(k == 7))
                return ins
            prog.on("tensor", mkg)
            prog.phase()
            prog.on("vector", lambda e, _b=bank: e.tensor_copy(s_sb[:], _b[0:128, 0:8]))
            prog.phase()
            prog.on("vector", lambda e: e.reduce_max(mxg[:], s_sb[:], axis=AX.X))
            prog.phase()
            prog.on("vector", lambda e: e.tensor_scalar_mul(mxg[:], mxg[:], -1.0))
            prog.phase()
            prog.on("scalar", lambda e: e.activation(s_sb[:], s_sb[:], AF.Exp, bias=mxg[:]))
            prog.phase()
            prog.on("vector", lambda e: e.tensor_reduce(smg[:], s_sb[:], axis=AX.X, op=ALU.add))
            prog.phase()
            prog.on("vector", lambda e: e.reciprocal(smg[:], smg[:]))
            prog.phase()
            prog.on("vector", lambda e: e.tensor_scalar_mul(s_sb[:], s_sb[:], smg[:]))
            prog.phase()
            prog.on("vector", lambda e: e.tensor_add(t1g[:], s_sb[:], gbt[:]))
            prog.phase()
            prog.on("vector", lambda e: e.reduce_max(mxg[:], t1g[:], axis=AX.X))
            prog.phase()
            prog.on("vector", lambda e: e.tensor_scalar(t2g[:], t1g[:], mxg[:], None, ALU.is_ge))
            prog.phase()
            prog.on("vector", lambda e: e.scalar_tensor_tensor(t2g[:], t2g[:], -1e30, t1g[:], ALU.mult, ALU.add))
            prog.phase()
            prog.on("vector", lambda e: e.reduce_max(mxg[:], t2g[:], axis=AX.X))
            prog.phase()
            prog.on("vector", lambda e: e.tensor_scalar(t2g[:], t1g[:], mxg[:], None, ALU.is_ge))
            prog.phase()
            prog.on("vector", lambda e: e.tensor_mul(c_sb[:], s_sb[:], t2g[:]))
            prog.phase()
            prog.dma("gpsimd", combo[ti], c_sb[:])
            prog.phase()
    return prog.build()


def stage_front(kind, x_fm, wmap, bmap=None, extra=None):
    nc = get_prog("front", build_front, kind)
    in_maps = []
    for c in range(NCORES):
        m = {"x": x_fm[c]}
        for n, w in wmap.items():
            m[n] = w
        if bmap:
            for n, b in bmap.items():
                m["b" + n] = b
        if extra:
            for n, v in extra.items():
                m[n] = v[c] if isinstance(v, list) else v
        in_maps.append(m)
    return run8(nc, in_maps)


def l0_block_apply(inp, x_fm):
    WqT = lhsT_tiles(np.ascontiguousarray(inp["L0_attn_Wq"].T.astype(np.float32)))
    WkT = lhsT_tiles(np.ascontiguousarray(inp["L0_attn_Wk"].T.astype(np.float32)))
    WvT = lhsT_tiles(np.ascontiguousarray(inp["L0_attn_Wv"].T.astype(np.float32)))
    idx = (np.pi / 2 * np.arange(1, L + 1, dtype=np.float64) / L)
    sin = np.sin(idx).astype(np.float32)
    cos = np.cos(idx).astype(np.float32)
    sinb = [np.ascontiguousarray(np.tile(sin[(c % 2) * T:(c % 2 + 1) * T], (128, 1)))
            for c in range(NCORES)]
    cosb = [np.ascontiguousarray(np.tile(cos[(c % 2) * T:(c % 2 + 1) * T], (128, 1)))
            for c in range(NCORES)]
    def btile(b):
        return np.ascontiguousarray(b.astype(np.float32).reshape(8, 128, 1))
    gwT = lhsT_tiles(np.ascontiguousarray(inp["L0_ffn_gate_w"].T.astype(np.float32)))
    gbb = np.ascontiguousarray(np.tile(inp["L0_ffn_gate_b"].astype(np.float32), (128, 1)))
    res = stage_front("l0", x_fm,
                      {"wq": WqT, "wk": WkT, "wv": WvT, "gw": gwT},
                      {"wq": btile(inp["L0_attn_bq"]), "wk": btile(inp["L0_attn_bk"]),
                       "wv": btile(inp["L0_attn_bv"])},
                      {"sinb": sinb, "cosb": cosb, "gb": gbb})
    xn = [r["xn"] for r in res]
    comb = [r["comb"].reshape(T, 8) for r in res]
    attn = stage_mix0([r["oqs"] for r in res], [r["oqc"] for r in res],
                      [r["oks"] for r in res], [r["okc"] for r in res],
                      [r["owv"] for r in res])
    attn_fm = [to_fm(p) for p in shard_tokens(attn)]
    WoT = np.ascontiguousarray(inp["L0_attn_Wo"].T)
    xmid = stage_proj(attn_fm, WoT, act="none", bias=inp["L0_attn_bo"],
                      resid_parts_fm=x_fm, stream_out=True)
    return xn, comb, xmid


def build_gq():
    nc = bass.Bass("TRN2", target_bir_lowering=False)
    w1 = nc.dram_tensor("gqw1", [8, 128, 4096], F32, kind="ExternalInput").ap()
    w2 = nc.dram_tensor("gqw2", [32, 128, 1024], F32, kind="ExternalInput").ap()
    q1 = nc.dram_tensor("q1", [8, 128, 4096], F32, kind="ExternalOutput").ap()
    q2 = nc.dram_tensor("q2", [32, 128, 1024], F32, kind="ExternalOutput").ap()
    prog = Prog(nc)
    ctx = Ctx(nc)
    wt = [sb(nc, f"gqwt{i}", 128, 4096) for i in range(2)]
    at = [sb(nc, f"gqa{i}", 128, 4096) for i in range(2)]
    st_ = [sb(nc, f"gqs{i}", 128, 4096) for i in range(2)]
    acc = sb(nc, "acc", 128, 2)
    r = sb(nc, "r", 128, 1)
    ones = sb(nc, "ones", 128, 1)
    onesr = sb(nc, "onesr", 1, 128)
    g_sb = sb(nc, "g_sb", 1, 2)
    th = sb(nc, "th", 128, 2)
    prog.on("gpsimd", lambda e: e.memset(acc[:], 0.0))
    prog.on("gpsimd", lambda e: e.memset(ones[:], 1.0))
    prog.on("gpsimd", lambda e: e.memset(onesr[:], 1.0))
    mats = [(w1, q1, 8, 4096, 0), (w2, q2, 32, 1024, 1)]
    for (wd, qd, kt, F, col) in mats:
        prog.phase()
        prog.dma("sync", wt[0][:, 0:F], wd[0])
        for k in range(kt):
            prog.phase()
            if k + 1 < kt:
                prog.dma("sync", wt[(k + 1) % 2][:, 0:F], wd[k + 1])
            prog.on("vector", lambda e, _k=k, _F=F: e.tensor_reduce(
                r[:], wt[_k % 2][:, 0:_F], axis=AX.X, op=ALU.add,
                apply_absolute_value=True))
            prog.phase()
            prog.on("vector", lambda e, _c=col: e.tensor_add(
                acc[:, _c:_c + 1], acc[:, _c:_c + 1], r[:]))
    prog.phase()
    b1_, b2_ = ctx.bank(), ctx.bank()
    def mmg(e, _b1=b1_, _b2=b2_):
        i = e.matmul(_b1[0:1, 0:1], acc[:, 0:1], ones[:, 0:1], start=True, stop=True)
        i = e.matmul(_b2[0:1, 0:1], acc[:, 1:2], ones[:, 0:1], start=True, stop=True)
        return i
    prog.on("tensor", mmg)
    prog.phase()
    prog.on("scalar", lambda e, _b=b1_: e.activation(g_sb[0:1, 0:1], _b[0:1, 0:1],
                                                     AF.Copy, scale=1.0 / (1024 * 4096)))
    prog.on("scalar", lambda e, _b=b2_: e.activation(g_sb[0:1, 1:2], _b[0:1, 0:1],
                                                     AF.Copy, scale=1.0 / (4096 * 1024)))
    prog.phase()
    prog.on("vector", lambda e: e.tensor_scalar_add(g_sb[:], g_sb[:], 1e-6))
    prog.phase()
    bank = ctx.bank()
    prog.on("tensor", lambda e, _b=bank: e.matmul(_b[0:128, 0:2], onesr[0:1, :],
                                                  g_sb[0:1, 0:2], start=True, stop=True))
    prog.phase()
    prog.on("scalar", lambda e, _b=bank: e.activation(th[:], _b[0:128, 0:2], AF.Copy, scale=0.5))
    for (wd, qd, kt, F, col) in mats:
        prog.phase()
        prog.dma("sync", wt[0][:, 0:F], wd[0])
        for k in range(kt):
            prog.phase()
            if k + 1 < kt:
                prog.dma("sync", wt[(k + 1) % 2][:, 0:F], wd[k + 1])
            prog.on("scalar", lambda e, _k=k, _F=F: e.activation(at[_k % 2][:, 0:_F], wt[_k % 2][:, 0:_F], AF.Abs))
            prog.on("scalar", lambda e, _k=k, _F=F: e.sign(st_[_k % 2][:, 0:_F], wt[_k % 2][:, 0:_F]))
            prog.phase()
            prog.on("vector", lambda e, _k=k, _F=F, _c=col: e.tensor_scalar(
                at[_k % 2][:, 0:_F], at[_k % 2][:, 0:_F], th[:, _c:_c + 1], None, ALU.is_gt))
            prog.phase()
            prog.on("vector", lambda e, _k=k, _F=F: e.tensor_mul(
                st_[_k % 2][:, 0:_F], at[_k % 2][:, 0:_F], st_[_k % 2][:, 0:_F]))
            prog.phase()
            prog.dma("gpsimd", qd[k], st_[k % 2][:, 0:F])
    return prog.build()
